# version 19
# baseline (speedup 1.0000x reference)
import numpy as np
import ml_dtypes
import concourse.bass as bass
import concourse.mybir as mybir
from concourse.bass_utils import run_bass_kernel_spmd

F32 = mybir.dt.float32
BF16 = mybir.dt.bfloat16
AF = mybir.ActivationFunctionType
ALU = mybir.AluOpType

DM = 1024
DI = 2048
H = 32
PH = 64
NS = 64
DC = 4
CS = 128
T = 2048
TQ = 512
NQ = T // TQ
CHQ = TQ // CS
DIN = 2 * DI + 2 * NS + H
DH = 4096
EPS = 1e-5
NCORES = 8
N_DMT = DM // 128
N_DHT = DH // 128
XW = 4 + TQ
TF = 1024

INP_BLOCKS = [128] * 16 + [128] * 16 + [64, 64, 32]



class Res:
    __slots__ = ("name", "writer", "readers")

    def __init__(self, name):
        self.name = name
        self.writer = None
        self.readers = []


class Op:
    __slots__ = ("idx", "eng", "fn", "deps", "is_dma", "sem", "semval",
                 "need_inc", "know", "tag")

    def __init__(self, idx, eng, fn, is_dma, tag):
        self.idx = idx
        self.eng = eng
        self.fn = fn
        self.deps = []
        self.is_dma = is_dma
        self.sem = None
        self.semval = None
        self.need_inc = False
        self.know = None
        self.tag = tag


class Prog:
    ENGS = ("PE", "DVE", "ACT", "POOL", "SP")

    def __init__(self):
        self.ops = []
        self.dma_sems = {}

    def op(self, eng, fn, reads=(), writes=(), tag=""):
        o = Op(len(self.ops), eng, fn, False, tag)
        self._wire(o, reads, writes)
        self.ops.append(o)
        return o

    def dma(self, queue, sem_name, fn, reads=(), writes=(), tag=""):
        o = Op(len(self.ops), queue, fn, True, tag)
        o.sem = sem_name
        self.dma_sems[sem_name] = self.dma_sems.get(sem_name, 0) + 16
        o.semval = self.dma_sems[sem_name]
        self._wire(o, reads, writes)
        self.ops.append(o)
        return o

    def _wire(self, o, reads, writes):
        dep = set()
        for r in reads:
            if r.writer is not None:
                dep.add(r.writer)
        for w in writes:
            if w.writer is not None:
                dep.add(w.writer)
            for rd in w.readers:
                dep.add(rd)
        dep.discard(o)
        o.deps = sorted(dep, key=lambda x: x.idx)
        for r in reads:
            r.readers.append(o)
        for w in writes:
            w.writer = o
            w.readers = []

    def finalize(self):
        def needs_edge(d, o):
            if d.is_dma:
                return True
            if d.eng == o.eng and d.eng == "PE":
                return False
            return True

        for o in self.ops:
            for d in o.deps:
                if (not d.is_dma) and needs_edge(d, o):
                    d.need_inc = True
        count = {e: 0 for e in self.ENGS}
        for o in self.ops:
            if (not o.is_dma) and o.need_inc:
                count[o.eng] += 1
                o.semval = count[o.eng]
        self.final_counts = dict(count)
        know = {e: {} for e in self.ENGS}
        waits = []
        for o in self.ops:
            k = know[o.eng]
            w = {}
            for d in o.deps:
                if not needs_edge(d, o):
                    continue
                if d.is_dma:
                    key, val = ("D", d.sem), d.semval
                else:
                    key, val = ("E", d.eng), d.semval
                    if val is None:
                        continue
                if k.get(key, 0) >= val:
                    continue
                if w.get(key, 0) < val:
                    w[key] = val
                for kk, vv in d.know.items():
                    if k.get(kk, 0) < vv:
                        k[kk] = vv
                k[key] = max(k.get(key, 0), val)
            waits.append(w)
            ksnap = dict(k)
            if o.is_dma:
                ksnap[("D", o.sem)] = o.semval
            elif o.need_inc:
                ksnap[("E", o.eng)] = o.semval
            o.know = ksnap
        self.waits = waits



def _blocked_wT(w, widths=None):
    wt = np.ascontiguousarray(np.asarray(w).T)
    if widths is None:
        widths = [128] * (np.asarray(w).shape[0] // 128)
    parts, o = [], 0
    for ow in widths:
        parts.append(np.ascontiguousarray(wt[:, o:o + ow]).reshape(-1))
        o += ow
    return np.concatenate(parts)


def _prep_inputs(inputs):
    g = {}
    for i in range(2):
        g[f"inw{i}"] = _blocked_wT(inputs["L0_attn_in_w"][i], INP_BLOCKS)
        g[f"outw{i}"] = _blocked_wT(inputs["L0_attn_out_w"][i])
        cw = np.asarray(inputs["L0_attn_conv_w"][i])
        g[f"convwx{i}"] = np.ascontiguousarray(
            cw[:DI].reshape(16, 128, DC).transpose(1, 0, 2).reshape(128, 16 * DC))
        g[f"convwbc{i}"] = np.ascontiguousarray(
            np.concatenate([cw[DI:DI + NS], cw[DI + NS:]], axis=1))
        cb = np.asarray(inputs["L0_attn_conv_b"][i])
        g[f"convbx{i}"] = np.ascontiguousarray(cb[:DI].reshape(16, 128).T)
        g[f"convbbc{i}"] = np.ascontiguousarray(
            np.stack([cb[DI:DI + NS], cb[DI + NS:]], axis=1))
        g[f"dtb{i}"] = np.ascontiguousarray(inputs["L0_attn_dt_bias"][i]).reshape(H, 1)
        g[f"alog{i}"] = np.ascontiguousarray(inputs["L0_attn_A_log"][i]).reshape(H, 1)
        g[f"drep{i}"] = np.ascontiguousarray(
            np.repeat(np.asarray(inputs["L0_attn_Dp"][i]), PH).reshape(16, 128).T)
        g[f"rmsw{i}"] = np.ascontiguousarray(
            np.asarray(inputs["L0_attn_rms_w"][i]).reshape(16, 128).T)
        g[f"normw{i}"] = np.ascontiguousarray(
            np.asarray(inputs["L0_attn_norm_w"][i]).reshape(8, 128).T)
    for nm, key in (("w1", "L0_ffn_w1_w"), ("w2", "L0_ffn_w2_w"),
                    ("h1", "L0_ffn_h1_w"), ("h2", "L0_ffn_h2_w"),
                    ("w3", "L0_ffn_w3_w")):
        g[nm] = _blocked_wT(inputs[key])
    for nm, key in (("s1", "L0_ffn_w1_s"), ("s2", "L0_ffn_w2_s"),
                    ("sh1", "L0_ffn_h1_s"), ("sh2", "L0_ffn_h2_s"),
                    ("s3", "L0_ffn_w3_s")):
        g[nm] = np.ascontiguousarray(np.asarray(inputs[key]).reshape(-1, 128).T)
    g["epsc"] = np.full((1, 1), EPS, dtype=np.float32)
    g["identb"] = np.eye(128, dtype=ml_dtypes.bfloat16)
    g["identf"] = np.eye(128, dtype=np.float32)
    g["onesb"] = np.ones((128, 128), dtype=ml_dtypes.bfloat16)
    m01 = np.zeros((128, 128), dtype=ml_dtypes.bfloat16)
    for j in range(128):
        m01[j, j:] = 1
    g["mask01"] = m01
    sg = np.zeros((H, 4, 512), dtype=ml_dtypes.bfloat16)
    for q in range(4):
        for hh in range(8):
            sg[q * 8 + hh, q, hh * 64:(hh + 1) * 64] = 1
    g["sgrp"] = np.ascontiguousarray(sg.reshape(H, 4 * 512))

    u = np.asarray(inputs["L0_attn_u"])
    per_core = []
    for c in range(NCORES):
        d = dict(g)
        d["u"] = np.ascontiguousarray(u[c].T)
        per_core.append(d)
    return per_core



def build_program(debug_taps=()):
    nc = bass.Bass("TRN2", target_bir_lowering=False)
    P = Prog()

    dr = {}
    def din(name, shape, dt=F32):
        dr[name] = nc.dram_tensor(name, shape, dt, kind="ExternalInput")

    din("u", [DM, T])
    for i in range(2):
        din(f"inw{i}", [DM * DIN]); din(f"outw{i}", [DI * DM])
        din(f"convwx{i}", [128, 16 * DC]); din(f"convwbc{i}", [NS, 2 * DC])
        din(f"convbx{i}", [128, 16]); din(f"convbbc{i}", [NS, 2])
        din(f"dtb{i}", [H, 1]); din(f"alog{i}", [H, 1])
        din(f"drep{i}", [128, 16]); din(f"rmsw{i}", [128, 16]); din(f"normw{i}", [128, 8])
    din("w1", [DM * DH]); din("w2", [DM * DH])
    din("h1", [DH * DH]); din("h2", [DH * DH]); din("w3", [DH * DM])
    for nm in ("s1", "s2", "sh1", "sh2"):
        din(nm, [128, DH // 128])
    din("s3", [128, DM // 128])
    din("epsc", [1, 1])
    din("identb", [128, 128], BF16); din("identf", [128, 128])
    din("onesb", [128, 128], BF16); din("mask01", [128, 128], BF16)
    din("sgrp", [H, 4 * 512], BF16)

    y_out = nc.dram_tensor("y", [DM, T], F32, kind="ExternalOutput")
    uspill = nc.dram_tensor("uspill", [DM, T], F32)
    dbg = {}
    for tname, shape in debug_taps:
        dbg[tname] = nc.dram_tensor("dbg_" + tname, list(shape), F32,
                                    kind="ExternalOutput")

    live_cms = {}
    def sbuf(name, shape, dt, side="left"):
        cm = nc.sbuf_tensor(name, shape, dt, side=side)
        h = cm.__enter__()
        live_cms[name] = cm
        return h

    def sbuf_free(name):
        live_cms.pop(name).__exit__(None, None, None)

    RESID = sbuf("RESID", [128, N_DMT * T], F32)
    STAT = sbuf("STAT", [1, 1024], F32)
    ULD = sbuf("ULD", [128, 512], F32)
    YSC = sbuf("YSC", [128, 512], F32)
    ESCR = sbuf("ESCR", [128, 1024], BF16)
    SQT = sbuf("SQT", [128, 512], BF16)
    WA = sbuf("WA", [128, 16 * 128], BF16)
    WB = sbuf("WB", [128, 16 * 128], BF16)
    WC = sbuf("WC", [128, 16 * 128], BF16)
    EPSC = sbuf("EPSC", [1, 1], F32)
    IDB = sbuf("IDB", [128, 128], BF16)
    IDF = sbuf("IDF", [128, 128], F32)
    ONESB = sbuf("ONESB", [128, 128], BF16)
    MASK01 = sbuf("MASK01", [128, 128], BF16)
    SGRP = sbuf("SGRP", [H, 4 * 512], BF16)
    ONES32F = sbuf("ONES32F", [32, CS], F32)
    FSC = {nm: sbuf("FSC_" + nm, [128, (DH if nm != "s3" else DM) // 128], F32)
           for nm in ("s1", "s2", "sh1", "sh2", "s3")}
    CWX = [sbuf(f"CWX{i}", [128, 16 * DC], F32) for i in range(2)]
    CWBC = [sbuf(f"CWBC{i}", [64, 2 * DC], F32) for i in range(2)]
    CBX = [sbuf(f"CBX{i}", [128, 16], F32) for i in range(2)]
    CBBC = [sbuf(f"CBBC{i}", [64, 2], F32) for i in range(2)]
    DTB = [sbuf(f"DTB{i}", [H, 1], F32) for i in range(2)]
    ALOG = [sbuf(f"ALOG{i}", [H, 1], F32) for i in range(2)]
    AEXP = [sbuf(f"AEXP{i}", [H, 1], F32) for i in range(2)]
    DCOL = [sbuf(f"DCOL{i}", [128, 16], F32) for i in range(2)]
    RMSW = [sbuf(f"RMSW{i}", [128, 16], F32) for i in range(2)]
    NORMW = [sbuf(f"NORMW{i}", [128, 8], F32) for i in range(2)]

    SZ = sbuf("SZ", [128, 16 * TQ], BF16, side="right")
    XBC = sbuf("XBC", [128, 16 * TQ], BF16, side="right")
    ARN = sbuf("ARN", [128, 16 * XW], BF16, side="right")
    NRM = sbuf("NRM", [128, N_DMT * TQ], BF16, side="right")
    DIAG = sbuf("DIAG", [128, 64 * 128], BF16, side="right")
    BCRAW = sbuf("BCRAW", [64, 2 * XW], BF16, side="right")
    BCC = sbuf("BCC", [64, 2 * TQ], BF16, side="right")
    DGBC = sbuf("DGBC", [64, 8 * 64], BF16, side="right")
    DSTACK = sbuf("DSTACK", [128, TQ], F32, side="right")
    TMC = sbuf("TMC", [128, 4 * 192], F32, side="right")
    XDT = sbuf("XDT", [128, DI], BF16, side="right")
    XDD = sbuf("XDD", [128, DI], BF16, side="right")
    YTM = sbuf("YTM", [128, DI], BF16, side="right")
    GM = sbuf("GM", [128, 3 * 512], BF16, side="right")
    GTM = sbuf("GTM", [128, 4 * 128], BF16, side="right")
    BTM = sbuf("BTM", [128, 4 * NS], BF16, side="right")
    PREV = sbuf("PREV", [64, H * PH], F32, side="right")
    PREVB = sbuf("PREVB", [64, H * PH], BF16, side="right")
    CDR = sbuf("CDR", [32, 4], BF16, side="right")
    CDROW = sbuf("CDROW", [1, 2 * 512], BF16, side="right")

    PSB = [nc.psum_tensor(f"PSB{i}", [128, 512], F32).__enter__() for i in range(6)]
    PBB = [nc.psum_tensor(f"PBB{i}", [128, 1024], BF16).__enter__() for i in range(2)]

    R = {}
    def res(name):
        if name not in R:
            R[name] = Res(name)
        return R[name]

    rRES = res("RESID")
    rRQ = [res(f"RESQ{i}") for i in range(4)]
    rRESall = (rRES, rRQ[0], rRQ[1], rRQ[2], rRQ[3])
    rSZ = res("SZ"); rXBC = res("XBC"); rARN = res("ARN")
    rNRM = res("NRM"); rDIAG = res("DIAG"); rBCRAW = res("BCRAW"); rBCC = res("BCC")
    rDGBC = res("DGBC"); rDST = res("DSTACK"); rDA = res("DAROW"); rTMC = res("TMC")
    rXDT = res("XDT"); rXDD = res("XDD"); rYTM = res("YTM")
    rGM = [res("GM0"), res("GM1"), res("GM2")]
    rARNl = [res(f"ARN{i}") for i in range(16)]
    rNRMl = [res(f"NRM{i}") for i in range(8)]
    rARNall = (rARN,) + tuple(rARNl)
    rTMCl = [res(f"TMC{i}") for i in range(4)]
    rGTMl = [res(f"GTM{i}") for i in range(4)]
    rBTMl = [res(f"BTM{i}") for i in range(4)]
    rCDRl = [res(f"CDR{i}") for i in range(4)]
    rCDROW2 = res("CDROW2"); rMS2 = res("MSCR2"); rMS3 = res("MSCR3")
    rGTM = res("GTM"); rBTM = res("BTM"); rPREV = res("PREV"); rPREVB = res("PREVB")
    rMS = res("MSCR"); rES = res("ESCR"); rYS = res("YSC"); rCDR = res("CDR")
    rYSl = [res(f"YSC{i}") for i in range(4)]
    rCDROW = res("CDROW"); rSTAT = res("STAT"); rSQT = res("SQT"); rULD = res("ULD")
    rWA = res("WA"); rWB = res("WB"); rWC = res("WC")
    rPS = [res(f"PSB{i}") for i in range(6)]
    rPB = [res(f"PBB{i}") for i in range(2)]
    rCONST = res("CONST")
    rUSP = res("USPILL")
    rOUT = res("OUT")

    def setup_load(dst_ap, src_ap):
        P.dma("SP", "set", lambda e, d=dst_ap, s=src_ap: e.dma_start(d, s),
              reads=(), writes=(rCONST,), tag="setup")

    setup_load(EPSC[:, :], dr["epsc"].ap())
    setup_load(IDB[:, :], dr["identb"].ap())
    setup_load(IDF[:, :], dr["identf"].ap())
    setup_load(ONESB[:, :], dr["onesb"].ap())
    setup_load(MASK01[:, :], dr["mask01"].ap())
    setup_load(SGRP[:, :], dr["sgrp"].ap())
    for i in range(2):
        setup_load(CWX[i][:, :], dr[f"convwx{i}"].ap())
        setup_load(CWBC[i][:, :], dr[f"convwbc{i}"].ap())
        setup_load(CBX[i][:, :], dr[f"convbx{i}"].ap())
        setup_load(CBBC[i][:, :], dr[f"convbbc{i}"].ap())
        setup_load(DTB[i][:, :], dr[f"dtb{i}"].ap())
        setup_load(ALOG[i][:, :], dr[f"alog{i}"].ap())
        setup_load(DCOL[i][:, :], dr[f"drep{i}"].ap())
        setup_load(RMSW[i][:, :], dr[f"rmsw{i}"].ap())
        setup_load(NORMW[i][:, :], dr[f"normw{i}"].ap())
    for nm in ("s1", "s2", "sh1", "sh2", "s3"):
        setup_load(FSC[nm][:, :], dr[nm].ap())
    P.dma("SP", "set", lambda e: e.dma_start(
        RESID[:, :].rearrange("p (a t) -> p a t", t=T),
        dr["u"].ap().rearrange("(a p) t -> p a t", p=128)),
        reads=(), writes=rRESall + (rCONST,), tag="uload")

    P.op("DVE", lambda e: e.memset(ONES32F[:, :], 1.0),
         reads=(rCONST,), writes=(rCONST,), tag="ones32")
    for i in range(2):
        P.op("ACT", lambda e, i=i: e.activation(AEXP[i][:, :], ALOG[i][:, :], AF.Exp),
             reads=(rCONST,), writes=(rCONST,), tag="aexp")

    def dbg_store(name, sb_ap, reads):
        if name in dbg:
            P.dma("POOL", "dbgout", lambda e, d=dbg[name], s=sb_ap: e.dma_start(d.ap(), s),
                  reads=tuple(reads), writes=(rOUT,), tag="dbg")

    def wblk_load(wb, rwb, dram, elem_off, K, ow, extra_reads=(), tag="wl"):
        kt = K // 128
        src = bass.AP(dram, elem_off, [[ow, 128], [128 * ow, kt], [1, ow]])
        dst = wb[:, 0:kt * ow].rearrange("p (k o) -> p k o", o=ow)
        P.dma("POOL", "wb_" + rwb.name,
              lambda e, d=dst, s=src: e.dma_start(d, s),
              reads=tuple(extra_reads), writes=(rwb,), tag=tag)

    def rank1_bcast(row_ap, bank, reads, tag):
        P.op("PE", lambda e, r=row_ap: e.matmul(
            PSB[bank][:, 0:512], ONESB[0:1, :], r, start=True, stop=True),
            reads=tuple(reads) + (rCONST,), writes=(rPS[bank],), tag=tag)

    STATB = YSC[0:1, 0:512].bitcast(BF16)

    def rownorm_istd(nelem, tag):
        P.op("ACT", lambda e: e.activation(
            STAT[0:1, 0:512], STAT[0:1, 0:512], AF.Ln, bias=EPSC[0:1, 0:1],
            scale=1.0 / nelem),
            reads=(rSTAT, rCONST), writes=(rSTAT,), tag=tag + "ln")
        P.op("ACT", lambda e: e.activation(
            STATB[0:1, 512:1024], STAT[0:1, 0:512], AF.Exp, scale=-0.5),
            reads=(rSTAT,),
            writes=(rSTAT, rYS, rYSl[0], rYSl[1], rYSl[2], rYSl[3]),
            tag=tag + "exp")

    def layernorm(dst_fn, tag):
        for tp in range(T // 512):
            t0 = tp * 512
            for k in range(N_DMT):
                P.op("DVE", lambda e, k=k, t0=t0: e.tensor_tensor(
                    SQT[:, 0:512], RESID[:, k * T + t0:k * T + t0 + 512],
                    RESID[:, k * T + t0:k * T + t0 + 512], ALU.mult),
                    reads=(rRQ[tp],), writes=(rSQT,), tag=tag + "sq")
                P.op("DVE", lambda e, k=k, t0=t0: e.tensor_copy(
                    ESCR[:, 0:512], RESID[:, k * T + t0:k * T + t0 + 512]),
                    reads=(rRQ[tp],), writes=(rES, rMS, rMS2), tag=tag + "cp")
                P.op("PE", lambda e, k=k: e.matmul(
                    PSB[2][0:1, 0:512], ONESB[:, 0:1], SQT[:, 0:512],
                    start=(k == 0), stop=(k == N_DMT - 1)),
                    reads=(rSQT, rCONST), writes=(rPS[2],), tag=tag + "mmsq")
                P.op("PE", lambda e, k=k: e.matmul(
                    PSB[5][0:1, 0:512], ONESB[:, 0:1], ESCR[:, 0:512],
                    start=(k == 0), stop=(k == N_DMT - 1)),
                    reads=(rES, rCONST), writes=(rPS[5],), tag=tag + "mmsum")
            P.op("DVE", lambda e: e.tensor_copy(STAT[0:1, 0:512], PSB[2][0:1, 0:512]),
                 reads=(rPS[2],), writes=(rSTAT,), tag=tag + "evA")
            P.op("DVE", lambda e: e.tensor_scalar(
                STAT[0:1, 512:1024], PSB[5][0:1, 0:512], 1.0 / DM, None, ALU.mult),
                reads=(rPS[5],), writes=(rSTAT,), tag=tag + "mean")
            P.op("DVE", lambda e: e.tensor_tensor(
                YSC[0:1, 0:512], STAT[0:1, 512:1024], STAT[0:1, 512:1024],
                ALU.mult),
                reads=(rSTAT,), writes=(rYS,), tag=tag + "m2")
            P.op("DVE", lambda e: e.scalar_tensor_tensor(
                STAT[0:1, 0:512], STAT[0:1, 0:512], 1.0 / DM,
                YSC[0:1, 0:512], ALU.mult, ALU.subtract),
                reads=(rSTAT, rYS), writes=(rSTAT,), tag=tag + "var")
            P.op("DVE", lambda e: e.tensor_copy(
                STATB[0:1, 0:512], STAT[0:1, 512:1024]),
                reads=(rSTAT,),
                writes=(rSTAT, rYS, rYSl[0], rYSl[1], rYSl[2], rYSl[3]),
                tag=tag + "meanb")
            rownorm_istd(1, tag)
            rank1_bcast(STATB[0:1, 512:1024], 3, (rSTAT, rYS), tag + "bi")
            rank1_bcast(STATB[0:1, 0:512], 4, (rSTAT, rYS), tag + "bm")
            for k in range(N_DMT):
                P.op("DVE", lambda e, k=k, t0=t0: e.tensor_tensor(
                    ULD[:, 0:512], RESID[:, k * T + t0:k * T + t0 + 512],
                    PSB[4][:, 0:512], ALU.subtract),
                    reads=(rRQ[tp], rPS[4]), writes=(rULD,), tag=tag + "sub")
                dst_fn(k, t0, tp)

    def attn_layer(li):
        inw = dr[f"inw{li}"]; outw = dr[f"outw{li}"]
        P.op("DVE", lambda e: e.memset(PREV[0:64, :], 0.0),
             reads=(), writes=(rPREV,), tag="prev0")
        P.op("DVE", lambda e: e.memset(PREVB[0:64, :], 0.0),
             reads=(), writes=(rPREVB,), tag="prevb0")
        for t in range(16):
            for w in range(DC):
                P.op("ACT", lambda e, t=t, w=w: e.activation(
                    DIAG[:, (t * DC + w) * 128:(t * DC + w + 1) * 128],
                    IDB[:, :], AF.Copy,
                    scale=CWX[li][:, t * DC + w:t * DC + w + 1]),
                    reads=(rCONST,), writes=(rDIAG,), tag="diagx")
        for bc in range(2):
            for w in range(DC):
                P.op("ACT", lambda e, bc=bc, w=w: e.activation(
                    DGBC[0:64, (bc * DC + w) * 64:(bc * DC + w + 1) * 64],
                    IDB[0:64, 0:64], AF.Copy,
                    scale=CWBC[li][0:64, bc * DC + w:bc * DC + w + 1]),
                    reads=(rCONST,), writes=(rDGBC,), tag="diagbc")

        blk_off = [0]
        for ow in INP_BLOCKS:
            blk_off.append(blk_off[-1] + DM * ow)

        for qi in range(NQ):
            t0 = qi * TQ
            for k in range(N_DMT):
                P.op("DVE", lambda e, k=k, t0=t0: e.tensor_tensor(
                    SQT[:, 0:512], RESID[:, k * T + t0:k * T + t0 + 512],
                    RESID[:, k * T + t0:k * T + t0 + 512], ALU.mult),
                    reads=(rRQ[qi],), writes=(rSQT,), tag="rmsq")
                P.op("PE", lambda e, k=k: e.matmul(
                    PSB[2][0:1, 0:512], ONESB[:, 0:1], SQT[:, 0:512],
                    start=(k == 0), stop=(k == N_DMT - 1)),
                    reads=(rSQT, rCONST), writes=(rPS[2],), tag="rmsmm")
            P.op("DVE", lambda e: e.tensor_copy(STAT[0:1, 0:512], PSB[2][0:1, 0:512]),
                 reads=(rPS[2],), writes=(rSTAT,), tag="rmsev")
            rownorm_istd(DM, "rms")
            rank1_bcast(STATB[0:1, 512:1024], 3, (rSTAT, rYS), "rmsbc")
            for k in range(N_DMT):
                P.op("DVE", lambda e, k=k, t0=t0: e.scalar_tensor_tensor(
                    NRM[:, k * TQ:(k + 1) * TQ],
                    RESID[:, k * T + t0:k * T + t0 + 512],
                    NORMW[li][:, k:k + 1], PSB[3][:, 0:512],
                    ALU.mult, ALU.mult),
                    reads=(rRQ[qi], rPS[3], rCONST),
                    writes=(rNRM, rNRMl[k]), tag="rmsap")
            dbg_store(f"normed{li}_{qi}", NRM[:, :], (rNRM,))

            if qi == 0:
                P.op("DVE", lambda e: e.memset(ARN[:, :], 0.0),
                     reads=(rARN,), writes=rARNall, tag="tail0")
                P.op("DVE", lambda e: e.memset(BCRAW[0:64, :], 0.0),
                     reads=(rBCRAW,), writes=(rBCRAW,), tag="tail0bc")

            groups = [(34,), (32, 33)] + [(2 * i, 2 * i + 1) for i in range(16)]
            wrot = [(WA, rWA), (WB, rWB), (WC, rWC)]

            def emit_inproj_group(gi, grp):
                wb, rwb = wrot[gi % 3]
                nb = len(grp)
                ow = INP_BLOCKS[grp[0]]
                src = bass.AP(inw, blk_off[grp[0]],
                              [[ow, 128], [DM * ow, nb], [128 * ow, DM // 128],
                               [1, ow]])
                dst = wb[:, 0:8 * nb * ow].rearrange("p (b k o) -> p b k o",
                                                     b=nb, o=ow)
                P.dma("POOL", "wb_" + rwb.name,
                      lambda e, d=dst, s=src: e.dma_start(d, s),
                      reads=(), writes=(rwb,), tag=f"inwg{gi}")
                for bi, ob in enumerate(grp):
                    pb, rpb = (PSB[0], rPS[0]) if ob % 2 == 0 else (PSB[1], rPS[1])
                    for k in range(N_DMT):
                        P.op("PE", lambda e, ow=ow, k=k, bi=bi, wb=wb, pb=pb:
                             e.matmul(pb[0:ow, 0:512],
                                      wb[:, (bi * 8 + k) * ow:(bi * 8 + k + 1) * ow],
                                      NRM[:, k * TQ:k * TQ + 512],
                                      start=(k == 0), stop=(k == N_DMT - 1)),
                             reads=(rwb, rNRMl[k]), writes=(rpb,), tag="inpmm")
                    if ob < 16:
                        P.op("ACT", lambda e, ob=ob, pb=pb: e.activation(
                            SZ[:, ob * TQ:(ob + 1) * TQ], pb[:, 0:512], AF.Silu),
                            reads=(rpb,), writes=(rSZ,), tag="zev")
                    elif ob < 32:
                        t = ob - 16
                        P.op("DVE", lambda e, t=t, pb=pb: e.tensor_copy(
                            ARN[:, t * XW + 4:t * XW + 4 + 512], pb[:, 0:512]),
                            reads=(rpb,), writes=(rARN, rARNl[t]), tag="xev")
                    elif ob < 34:
                        bc = ob - 32
                        P.op("DVE", lambda e, bc=bc, pb=pb: e.tensor_copy(
                            BCRAW[0:64, bc * XW + 4:bc * XW + 4 + 512],
                            pb[0:64, 0:512]),
                            reads=(rpb,), writes=(rBCRAW,), tag="bcev")
                    else:
                        P.op("DVE", lambda e, pb=pb: e.tensor_copy(
                            DSTACK[0:32, 0:512], pb[0:32, 0:512]),
                            reads=(rpb,), writes=(rDST,), tag="dtev")

            def emit_bcconv_dt():
                for bc in range(2):
                    pb, rpb = (PSB[4], rPS[4]) if bc == 0 else (PSB[5], rPS[5])
                    for w in range(DC):
                        P.op("PE", lambda e, bc=bc, w=w, pb=pb: e.matmul(
                            pb[0:64, 0:512],
                            DGBC[0:64, (bc * DC + w) * 64:(bc * DC + w + 1) * 64],
                            BCRAW[0:64, bc * XW + 1 + w:bc * XW + 1 + w + 512],
                            start=(w == 0), stop=(w == DC - 1)),
                            reads=(rDGBC, rBCRAW), writes=(rpb,), tag="convmmbc")
                    P.op("ACT", lambda e, bc=bc, pb=pb: e.activation(
                        BCC[0:64, bc * TQ:(bc + 1) * TQ], pb[0:64, 0:512],
                        AF.Silu, bias=CBBC[li][0:64, bc:bc + 1]),
                        reads=(rpb, rCONST), writes=(rBCC,), tag="convevbc")
                P.op("ACT", lambda e: e.activation(
                    YSC[0:32, :], DSTACK[0:32, :], AF.Abs, bias=DTB[li][:, :]),
                    reads=(rDST, rCONST),
                    writes=(rYS, rYSl[0], rYSl[1], rYSl[2], rYSl[3]),
                    tag="sp_abs")
                P.op("ACT", lambda e: e.activation(
                    YSC[0:32, :], YSC[0:32, :], AF.Exp, scale=-1.0),
                    reads=(rYS,), writes=(rYS,), tag="sp_exp")
                P.op("ACT", lambda e: e.activation(
                    YSC[0:32, :], YSC[0:32, :], AF.Ln, bias=1.0),
                    reads=(rYS,), writes=(rYS,), tag="sp_ln")
                P.op("ACT", lambda e: e.activation(
                    DSTACK[0:32, :], DSTACK[0:32, :], AF.Relu, bias=DTB[li][:, :]),
                    reads=(rDST, rCONST), writes=(rDST,), tag="sp_relu")
                P.op("DVE", lambda e: e.tensor_tensor(
                    DSTACK[0:32, :], DSTACK[0:32, :], YSC[0:32, :], ALU.add),
                    reads=(rDST, rYS), writes=(rDST,), tag="sp_add")
                P.op("DVE", lambda e: e.tensor_scalar(
                    YSC[0:32, :], DSTACK[0:32, :], AEXP[li][:, :], -1.0,
                    ALU.mult, ALU.mult),
                    reads=(rDST, rCONST), writes=(rYS,), tag="dA")

            def emit_front(c):
                c0, c1 = c * CS, (c + 1) * CS
                tmc = TMC[:, c * 192:c * 192 + 192]
                rtmc = rTMCl[c]
                gtm = GTM[:, c * 128:c * 128 + 128]
                rgtm = rGTMl[c]
                btm = BTM[:, c * NS:c * NS + NS]
                rbtm = rBTMl[c]
                cdr = CDR[0:32, c:c + 1]
                rcdr = rCDRl[c]
                P.op("DVE", lambda e, c0=c0, c1=c1: e.tensor_tensor_scan(
                    DSTACK[32:64, c0:c1], ONES32F[:, :], YSC[0:32, c0:c1],
                    0.0, ALU.mult, ALU.add),
                    reads=(rYS, rCONST), writes=(rDST,), tag="cumsum")
                P.op("ACT", lambda e, c0=c0, c1=c1: e.activation(
                    DSTACK[64:96, c0:c1], DSTACK[32:64, c0:c1], AF.Exp,
                    scale=-1.0, bias=DSTACK[32:64, c1 - 1:c1]),
                    reads=(rDST,), writes=(rDST,), tag="decay")
                P.op("ACT", lambda e, c0=c0, c1=c1: e.activation(
                    DSTACK[96:128, c0:c1], DSTACK[32:64, c0:c1], AF.Exp),
                    reads=(rDST,), writes=(rDST,), tag="ecum")
                P.op("PE", lambda e, c0=c0, c1=c1: e.transpose(
                    PSB[4][:, 0:128], DSTACK[:, c0:c1], IDF[:, :]),
                    reads=(rDST, rCONST), writes=(rPS[4],), tag="stackT")
                P.op("DVE", lambda e, tmc=tmc: e.tensor_copy(
                    tmc[:, 0:128], PSB[4][:, 0:128]),
                    reads=(rPS[4],), writes=(rtmc,), tag="stackEv")
                P.op("DVE", lambda e, tmc=tmc: e.tensor_tensor(
                    tmc[:, 128:160], tmc[:, 0:32], tmc[:, 64:96], ALU.mult),
                    reads=(rtmc,), writes=(rtmc,), tag="dtdecay")
                P.op("ACT", lambda e, tmc=tmc: e.activation(
                    tmc[:, 160:192], tmc[:, 0:32], AF.Ln),
                    reads=(rtmc,), writes=(rtmc,), tag="lndt")
                P.op("DVE", lambda e, tmc=tmc: e.tensor_tensor(
                    tmc[:, 160:192], tmc[:, 160:192], tmc[:, 32:64],
                    ALU.subtract),
                    reads=(rtmc,), writes=(rtmc,), tag="bias2")
                P.op("DVE", lambda e, c1=c1, cdr=cdr: e.tensor_copy(
                    cdr, DSTACK[96:128, c1 - 1:c1]),
                    reads=(rDST,), writes=(rcdr,), tag="cd")
                P.op("PE", lambda e, c0=c0, c1=c1: e.matmul(
                    PSB[5][:, 0:128], BCC[0:64, c0:c1],
                    BCC[0:64, TQ + c0:TQ + c1], start=True, stop=True),
                    reads=(rBCC,), writes=(rPS[5],), tag="gt")
                P.op("DVE", lambda e, gtm=gtm: e.tensor_tensor(
                    gtm, PSB[5][:, 0:128], MASK01[:, :], ALU.mult),
                    reads=(rPS[5], rCONST), writes=(rgtm,), tag="gtev")
                P.op("PE", lambda e, c0=c0, c1=c1: e.transpose(
                    PBB[1][:, 0:64], BCC[0:64, c0:c1], IDB[0:64, 0:64]),
                    reads=(rBCC, rCONST), writes=(rPB[1],), tag="btmT")
                P.op("DVE", lambda e, btm=btm: e.tensor_copy(btm, PBB[1][:, 0:64]),
                     reads=(rPB[1],), writes=(rbtm,), tag="btmEv")

            for gi, grp in enumerate(groups):
                emit_inproj_group(gi, grp)
                if gi == 1:
                    emit_bcconv_dt()
                if gi == 5:
                    for c in range(CHQ):
                        emit_front(c)

            for t in range(16):
                pb, rpb = (PSB[0], rPS[0]) if t % 2 == 0 else (PSB[1], rPS[1])
                for w in range(DC):
                    P.op("PE", lambda e, t=t, w=w, pb=pb: e.matmul(
                        pb[:, 0:512],
                        DIAG[:, (t * DC + w) * 128:(t * DC + w + 1) * 128],
                        ARN[:, t * XW + 1 + w:t * XW + 1 + w + 512],
                        start=(w == 0), stop=(w == DC - 1)),
                        reads=(rDIAG, rARNl[t]), writes=(rpb,), tag="convmm")
                P.op("ACT", lambda e, t=t, pb=pb: e.activation(
                    XBC[:, t * TQ:(t + 1) * TQ], pb[:, 0:512],
                    AF.Silu, bias=CBX[li][:, t:t + 1]),
                    reads=(rpb, rCONST), writes=(rXBC,), tag="convev")
            P.op("DVE", lambda e: e.tensor_copy(
                ARN[:, :].rearrange("p (t c) -> p t c", c=XW)[:, :, 1:4],
                ARN[:, :].rearrange("p (t c) -> p t c", c=XW)[:, :, XW - 3:XW]),
                reads=(rARN,), writes=rARNall, tag="tail")
            P.op("DVE", lambda e: e.tensor_copy(
                BCRAW[0:64, :].rearrange("p (t c) -> p t c", c=XW)[:, :, 1:4],
                BCRAW[0:64, :].rearrange("p (t c) -> p t c", c=XW)[:, :, XW - 3:XW]),
                reads=(rBCRAW,), writes=(rBCRAW,), tag="tailbc")
            dbg_store(f"convx{li}_{qi}", XBC[:, :], (rXBC,))

            for c in range(CHQ):
                c0, c1 = c * CS, (c + 1) * CS
                par = c % 2
                tmc = TMC[:, c * 192:c * 192 + 192]
                rtmc = rTMCl[c]
                gtm = GTM[:, c * 128:c * 128 + 128]
                rgtm = rGTMl[c]
                btm = BTM[:, c * NS:c * NS + NS]
                rbtm = rBTMl[c]
                cdr = CDR[0:32, c:c + 1]
                rcdr = rCDRl[c]
                cdrow = CDROW[0:1, par * 512:par * 512 + 512]
                rcdrow = rCDROW if par == 0 else rCDROW2
                for wv in range(2):
                    for dp in range(8):
                        d = wv * 8 + dp
                        P.op("PE", lambda e, d=d, dp=dp, c0=c0, c1=c1: e.transpose(
                            PBB[0][:, dp * 128:(dp + 1) * 128],
                            XBC[:, d * TQ + c0:d * TQ + c1], IDB[:, :]),
                            reads=(rXBC, rCONST), writes=(rPB[0],), tag="xT")
                    P.op("DVE", lambda e, wv=wv: e.tensor_copy(
                        XDT[:, wv * 1024:(wv + 1) * 1024], PBB[0][:, :]),
                        reads=(rPB[0],), writes=(rXDT,), tag="xev2")
                for h in range(H):
                    P.op("DVE", lambda e, h=h, tmc=tmc: e.tensor_scalar(
                        XDD[:, h * PH:(h + 1) * PH], XDT[:, h * PH:(h + 1) * PH],
                        tmc[:, 128 + h:128 + h + 1], None, ALU.mult),
                        reads=(rXDT, rtmc), writes=(rXDD,), tag="xdd")
                PBB1F = PBB[1][:, :].bitcast(F32)
                MB3 = [PSB[4], PSB[5], PBB1F]
                rMB3 = [rPS[4], rPS[5], rPB[1]]
                rMSl = [rMS, rMS2, rMS3]

                def mask_front(g):
                    mi = g % 3
                    mb, rmb = MB3[mi], rMB3[mi]
                    for h4 in range(4):
                        h = 4 * g + h4
                        P.op("PE", lambda e, h=h, h4=h4, mb=mb, tmc=tmc: e.transpose(
                            mb[:, h4 * 128:(h4 + 1) * 128],
                            tmc[:, 32 + h:32 + h + 1].broadcast_to([128, 128]),
                            IDF[:, :]),
                            reads=(rtmc, rCONST), writes=(rmb,), tag="mbc")

                def mask_mid(g):
                    mi = g % 3
                    mb, rmb = MB3[mi], rMB3[mi]
                    es = (ESCR[:, mi * 512:(mi + 1) * 512] if mi < 2
                          else SQT[:, 0:512])
                    res_ = rMSl[mi]
                    esw = (res_,) if mi < 2 else (res_, rSQT)
                    gmap = GM[:, mi * 512:(mi + 1) * 512]
                    rgm = rGM[mi]
                    for h4 in range(4):
                        h = 4 * g + h4
                        P.op("DVE", lambda e, h=h, h4=h4, mb=mb, tmc=tmc:
                             e.tensor_scalar(
                                 mb[:, h4 * 128:(h4 + 1) * 128],
                                 mb[:, h4 * 128:(h4 + 1) * 128],
                                 tmc[:, 160 + h:160 + h + 1], 30.0,
                                 ALU.add, ALU.min),
                             reads=(rmb, rtmc), writes=(rmb,), tag="md2")
                    P.op("ACT", lambda e, mb=mb, es=es: e.activation(
                        es, mb[:, 0:512], AF.Exp),
                        reads=(rmb,), writes=esw, tag="mexp")
                    P.op("DVE", lambda e, es=es, gmap=gmap, gtm=gtm:
                         e.tensor_tensor(
                             gmap.rearrange("p (r n) -> p r n", r=4),
                             es.rearrange("p (r n) -> p r n", r=4),
                             gtm.unsqueeze(1).broadcast_to([128, 4, 128]),
                             ALU.mult),
                         reads=esw + (rgtm,), writes=(rgm,), tag="mgm")


                def yoff_grp(g):
                    mi = g % 2
                    yob, ryo = (PSB[2 + mi], rPS[2 + mi])
                    for h4 in range(4):
                        h = 4 * g + h4
                        P.op("PE", lambda e, h=h, h4=h4, c0=c0, c1=c1, yob=yob:
                             e.matmul(yob[:, h4 * 64:(h4 + 1) * 64],
                                      BCC[0:64, TQ + c0:TQ + c1],
                                      PREVB[0:64, h * PH:(h + 1) * PH],
                                      start=True, stop=True),
                             reads=(rBCC, rPREVB), writes=(ryo,), tag="yoff")

                def ycsc_grp(g):
                    mi = g % 2
                    yob, ryo = (PSB[2 + mi], rPS[2 + mi])
                    ysc = YSC[:, mi * 256:(mi + 1) * 256]
                    for h4 in range(4):
                        h = 4 * g + h4
                        P.op("ACT", lambda e, h=h, h4=h4, yob=yob, ysc=ysc,
                             tmc=tmc: e.activation(
                                 ysc[:, h4 * 64:(h4 + 1) * 64],
                                 yob[:, h4 * 64:(h4 + 1) * 64], AF.Copy,
                                 scale=tmc[:, 96 + h:96 + h + 1]),
                             reads=(ryo, rtmc), writes=(rYSl[mi],), tag="ycsc")


                def ydiag_grp(g):
                    mi = g % 3
                    gmap = GM[:, mi * 512:(mi + 1) * 512]
                    rgm = rGM[mi]
                    ydb, ryd = (PSB[g % 2], rPS[g % 2])
                    for h4 in range(4):
                        h = 4 * g + h4
                        P.op("PE", lambda e, h=h, h4=h4, gmap=gmap, ydb=ydb:
                             e.matmul(ydb[:, h4 * 64:(h4 + 1) * 64],
                                      gmap[:, h4 * 128:(h4 + 1) * 128],
                                      XDT[:, h * PH:(h + 1) * PH],
                                      start=True, stop=True),
                             reads=(rgm, rXDT), writes=(ryd,), tag="ydiag")

                def ycomb_grp(g):
                    mi = g % 2
                    ydb, ryd = (PSB[g % 2], rPS[g % 2])
                    ysc = YSC[:, mi * 256:(mi + 1) * 256]
                    P.op("DVE", lambda e, g=g, ydb=ydb, ysc=ysc: e.tensor_tensor(
                        YTM[:, g * 256:(g + 1) * 256], ydb[:, 0:256], ysc,
                        ALU.add),
                        reads=(ryd, rYSl[mi]), writes=(rYTM,), tag="ycomb")

                mask_front(0); mask_front(1); mask_front(2)
                mask_mid(0); mask_mid(1)
                yoff_grp(0)
                ycsc_grp(0)
                for g in range(8):
                    ydiag_grp(g)
                    if g + 3 < 8:
                        mask_front(g + 3)
                    if g + 2 < 8:
                        mask_mid(g + 2)
                    if g + 1 < 8:
                        yoff_grp(g + 1)
                        ycsc_grp(g + 1)
                    ycomb_grp(g)
                for q in range(4):
                    sb_, rsb = (PSB[4 + q % 2], rPS[4 + q % 2])
                    cb_, rcb = (PSB[4 + (q + 1) % 2], rPS[4 + (q + 1) % 2])
                    for hh in range(8):
                        h = q * 8 + hh
                        P.op("PE", lambda e, h=h, hh=hh, sb_=sb_, btm=btm: e.matmul(
                            sb_[0:64, hh * 64:(hh + 1) * 64],
                            btm, XDD[:, h * PH:(h + 1) * PH],
                            start=True, stop=True),
                            reads=(rbtm, rXDD), writes=(rsb,), tag="states")
                    P.op("PE", lambda e, q=q, cb_=cb_, cdr=cdr: e.matmul(
                        cb_[0:1, 0:512], cdr,
                        SGRP[:, q * 512:(q + 1) * 512], start=True, stop=True),
                        reads=(rcdr, rCONST), writes=(rcb,), tag="cdrow")
                    P.op("DVE", lambda e, cb_=cb_, cdrow=cdrow: e.tensor_copy(
                        cdrow, cb_[0:1, 0:512]),
                        reads=(rcb,), writes=(rcdrow,), tag="cdrowev")
                    P.op("PE", lambda e, cb_=cb_, cdrow=cdrow: e.matmul(
                        cb_[0:64, 0:512], ONESB[0:1, 0:64], cdrow,
                        start=True, stop=True),
                        reads=(rcdrow, rCONST), writes=(rcb,), tag="cdrep")
                    P.op("DVE", lambda e, q=q, cb_=cb_: e.tensor_tensor(
                        YSC[0:64, 0:512], PREV[0:64, q * 512:(q + 1) * 512],
                        cb_[0:64, 0:512], ALU.mult),
                        reads=(rPREV, rcb),
                        writes=(rYS, rYSl[0], rYSl[1], rYSl[2], rYSl[3]),
                        tag="scan1")
                    P.op("DVE", lambda e, q=q, sb_=sb_: e.tensor_tensor(
                        PREV[0:64, q * 512:(q + 1) * 512], YSC[0:64, 0:512],
                        sb_[0:64, 0:512], ALU.add),
                        reads=(rYS, rYSl[0], rsb), writes=(rPREV,), tag="scan2")
                    P.op("ACT", lambda e, q=q: e.activation(
                        PREVB[0:64, q * 512:(q + 1) * 512],
                        PREV[0:64, q * 512:(q + 1) * 512], AF.Copy),
                        reads=(rPREV,), writes=(rPREVB,), tag="prevb")
                for wv in range(2):
                    for dp in range(8):
                        d = wv * 8 + dp
                        P.op("PE", lambda e, d=d, dp=dp: e.transpose(
                            PBB[1][:, dp * 128:(dp + 1) * 128],
                            YTM[:, d * 128:(d + 1) * 128], IDB[:, :]),
                            reads=(rYTM, rCONST), writes=(rPB[1],), tag="yT")
                    P.op("DVE", lambda e, wv=wv, c0=c0: e.tensor_copy(
                        ARN[:, :].rearrange("p (t c) -> p t c", c=XW)
                        [:, wv * 8:(wv + 1) * 8, 4 + c0:4 + c0 + 128],
                        PBB[1][:, :].rearrange("p (t c) -> p t c", c=128)),
                        reads=(rPB[1],),
                        writes=(rARN,) + tuple(rARNl[wv * 8:wv * 8 + 8]),
                        tag="yTev2")
            for d in range(16):
                P.op("DVE", lambda e, d=d: e.scalar_tensor_tensor(
                    ARN[:, d * XW + 4:d * XW + 4 + TQ],
                    XBC[:, d * TQ:(d + 1) * TQ], DCOL[li][:, d:d + 1],
                    ARN[:, d * XW + 4:d * XW + 4 + TQ], ALU.mult, ALU.add),
                    reads=(rXBC, rARNl[d], rCONST),
                    writes=(rARN, rARNl[d]), tag="yD")
                P.op("DVE", lambda e, d=d: e.tensor_tensor(
                    ARN[:, d * XW + 4:d * XW + 4 + TQ],
                    ARN[:, d * XW + 4:d * XW + 4 + TQ],
                    SZ[:, d * TQ:(d + 1) * TQ], ALU.mult),
                    reads=(rARNl[d], rSZ), writes=(rARN, rARNl[d]), tag="ygate")
            dbg_store(f"ygate{li}_{qi}", ARN[:, :], (rARN,))
            for d in range(16):
                P.op("DVE", lambda e, d=d: e.tensor_tensor(
                    SQT[:, 0:512], ARN[:, d * XW + 4:d * XW + 4 + TQ],
                    ARN[:, d * XW + 4:d * XW + 4 + TQ], ALU.mult),
                    reads=(rARNl[d],), writes=(rSQT,), tag="yrsq")
                P.op("PE", lambda e, d=d: e.matmul(
                    PSB[2][0:1, 0:512], ONESB[:, 0:1], SQT[:, 0:512],
                    start=(d == 0), stop=(d == 15)),
                    reads=(rSQT, rCONST), writes=(rPS[2],), tag="yrmm")
            P.op("DVE", lambda e: e.tensor_copy(STAT[0:1, 0:512], PSB[2][0:1, 0:512]),
                 reads=(rPS[2],), writes=(rSTAT,), tag="yrev")
            rownorm_istd(DI, "yr")
            rank1_bcast(STATB[0:1, 512:1024], 3, (rSTAT, rYS), "yrbc")
            for d in range(16):
                P.op("DVE", lambda e, d=d: e.scalar_tensor_tensor(
                    ARN[:, d * XW + 4:d * XW + 4 + TQ],
                    ARN[:, d * XW + 4:d * XW + 4 + TQ],
                    RMSW[li][:, d:d + 1], PSB[3][:, 0:512], ALU.mult, ALU.mult),
                    reads=(rARNl[d], rPS[3], rCONST),
                    writes=(rARN, rARNl[d]), tag="yrap")

            for ob in range(8):
                wb, rwb = (WA, rWA) if ob % 2 == 0 else (WB, rWB)
                wblk_load(wb, rwb, outw, ob * DI * 128, DI, 128, tag=f"outw{ob}")
                pb, rpb = (PSB[0], rPS[0]) if ob % 2 == 0 else (PSB[1], rPS[1])
                for k in range(16):
                    P.op("PE", lambda e, k=k, wb=wb, pb=pb: e.matmul(
                        pb[:, 0:512], wb[:, k * 128:(k + 1) * 128],
                        ARN[:, k * XW + 4:k * XW + 4 + 512],
                        start=(k == 0), stop=(k == 15)),
                        reads=(rwb, rARNl[k]), writes=(rpb,), tag="outmm")
                P.op("DVE", lambda e, ob=ob, t0=t0, pb=pb: e.tensor_tensor(
                    RESID[:, ob * T + t0:ob * T + t0 + 512],
                    RESID[:, ob * T + t0:ob * T + t0 + 512],
                    pb[:, 0:512], ALU.add),
                    reads=(rRQ[qi], rpb), writes=(rRQ[qi],), tag="outev")
            dbg_store(f"res{li}_{qi}", RESID[:, :], rRESall)

    P.dma("SP", "usp", lambda e: e.dma_start(
        uspill.ap().rearrange("(a p) t -> p a t", p=128),
        RESID[:, :].rearrange("p (a t) -> p a t", t=T)),
        reads=rRESall, writes=(rUSP,), tag="uspill")

    def ln1_dst(k, t0, tp):
        P.op("DVE", lambda e, k=k, t0=t0: e.tensor_tensor(
            RESID[:, k * T + t0:k * T + t0 + 512], ULD[:, 0:512],
            PSB[3][:, 0:512], ALU.mult),
            reads=(rULD, rPS[3], rUSP), writes=(rRQ[tp],), tag="ln1w")

    layernorm(ln1_dst, "ln1")
    dbg_store("ln1", RESID[:, :], rRESall)

    attn_layer(0)
    attn_layer(1)

    for k in range(N_DMT):
        for hp in range(4):
            P.dma("SP", "uld", lambda e, k=k, hp=hp: e.dma_start(
                ULD[:, :], bass.AP(uspill, k * 128 * T + hp * 512,
                                   [[T, 128], [1, 512]])),
                reads=(rUSP,), writes=(rULD,), tag="uld")
            P.op("DVE", lambda e, k=k, hp=hp: e.tensor_tensor(
                RESID[:, k * T + hp * 512:k * T + (hp + 1) * 512],
                RESID[:, k * T + hp * 512:k * T + (hp + 1) * 512],
                ULD[:, :], ALU.add),
                reads=(rRQ[hp], rULD), writes=(rRQ[hp],), tag="x1")
    dbg_store("x1", RESID[:, :], rRESall)

    rBAR = res("attn_done")
    P.op("DVE", lambda e: e.engine_nop(),
         reads=(rPREV, rPREVB, rCDR, rCDROW, rBTM, rGTM, rGM[0], rGM[1],
                rXDT, rXDD, rYTM, rTMC, rDST, rDA, rSZ, rXBC, rARN, rNRM,
                rDIAG, rBCRAW, rBCC, rDGBC),
         writes=(rBAR,), tag="attnbar")

    for nm in reversed(("SZ", "XBC", "ARN", "NRM", "DIAG", "BCRAW", "BCC",
                        "DGBC", "DSTACK", "TMC", "XDT", "XDD", "YTM",
                        "GM", "GTM", "BTM", "PREV", "PREVB", "CDR", "CDROW")):
        sbuf_free(nm)
    LN2B = sbuf("LN2B", [128, N_DMT * T], BF16, side="right")
    ABUF = sbuf("ABUF", [128, N_DHT * TF], BF16, side="right")
    WH = [sbuf(f"WH{i}", [128, 32 * 128], BF16, side="right") for i in range(2)]
    rLN2 = res("LN2B"); rAB = res("ABUF"); rWH = [res("WH0"), res("WH1")]
    rABl = [res(f"ABUF{i}") for i in range(8)]
    rGBl = [res(f"GB{i}") for i in range(8)]
    GBUF = RESID
    rGB = rRES

    def ln2_dst(k, t0, tp):
        P.op("DVE", lambda e, k=k, t0=t0: e.tensor_tensor(
            LN2B[:, k * T + t0:k * T + t0 + 512], ULD[:, 0:512],
            PSB[3][:, 0:512], ALU.mult),
            reads=(rULD, rPS[3], rUSP, rBAR), writes=(rLN2,), tag="ln2w")

    layernorm(ln2_dst, "ln2")
    dbg_store("ln2", LN2B[:, :], (rLN2,))

    P.dma("SP", "usp", lambda e: e.dma_start(
        uspill.ap().rearrange("(a p) t -> p a t", p=128),
        RESID[:, :].rearrange("p (a t) -> p a t", t=T)),
        reads=rRESall, writes=(rUSP,), tag="x1spill")

    def gsl(dt_, c0, n=512):
        return GBUF[:, :].bitcast(BF16)[:, dt_ * TF + c0:dt_ * TF + c0 + n]

    for ch in range(2):
        ct0 = ch * TF
        for stage in range(2):
            wnm, snm = (("w1", "s1"), ("w2", "s2"))[stage]
            for obp in range(N_DHT * 2):
                ob = obp
                if ob >= N_DHT:
                    break
                if ob % 2 == 0:
                    wb, rwb = (WA, rWA) if (ob // 2) % 2 == 0 else (WB, rWB)
                    src_ap = bass.AP(dr[wnm], ob * DM * 128,
                                     [[128, 128], [DM * 128, 2], [128 * 128, 8],
                                      [1, 128]])
                    dst_ap = wb[:, 0:2048].rearrange("p (b k o) -> p b k o",
                                                     b=2, o=128)
                    P.dma("POOL", "wb_" + rwb.name,
                          lambda e, d=dst_ap, s=src_ap: e.dma_start(d, s),
                          reads=(rUSP, rBAR), writes=(rwb,), tag=wnm)
                wb, rwb = (WA, rWA) if (ob // 2) % 2 == 0 else (WB, rWB)
                boff = (ob % 2) * 1024
                for nh in range(2):
                    pb, rpb = (PSB[0], rPS[0]) if nh == 0 else (PSB[1], rPS[1])
                    c0 = nh * 512
                    for k in range(N_DMT):
                        P.op("PE", lambda e, k=k, c0=c0, ct0=ct0, wb=wb, pb=pb,
                             boff=boff:
                             e.matmul(pb[:, 0:512],
                                      wb[:, boff + k * 128:boff + (k + 1) * 128],
                                      LN2B[:, k * T + ct0 + c0:k * T + ct0 + c0 + 512],
                                      start=(k == 0), stop=(k == N_DMT - 1)),
                             reads=(rwb, rLN2), writes=(rpb,), tag=wnm + "mm")
                    if stage == 0:
                        P.op("ACT", lambda e, ob=ob, c0=c0, pb=pb: e.activation(
                            ABUF[:, ob * TF + c0:ob * TF + c0 + 512], pb[:, 0:512],
                            AF.Silu, scale=FSC["s1"][:, ob:ob + 1]),
                            reads=(rpb, rCONST, rUSP, rBAR),
                            writes=(rAB, rABl[ob // 4]), tag="aev")
                    else:
                        P.op("DVE", lambda e, ob=ob, c0=c0, pb=pb:
                             e.scalar_tensor_tensor(
                                 ABUF[:, ob * TF + c0:ob * TF + c0 + 512],
                                 pb[:, 0:512], FSC["s2"][:, ob:ob + 1],
                                 ABUF[:, ob * TF + c0:ob * TF + c0 + 512],
                                 ALU.mult, ALU.mult),
                             reads=(rpb, rCONST, rABl[ob // 4]),
                             writes=(rAB, rABl[ob // 4]), tag="hev")
        for stage in range(2):
            wnm, snm = (("h1", "sh1"), ("h2", "sh2"))[stage]
            for ob in range(N_DHT):
                wb, rwb = WH[ob % 2], rWH[ob % 2]
                wblk_load(wb, rwb, dr[wnm], ob * DH * 128, DH, 128,
                          extra_reads=(rUSP, rBAR), tag=wnm)
                for nh in range(2):
                    pb, rpb = (PSB[0], rPS[0]) if nh == 0 else (PSB[1], rPS[1])
                    c0 = nh * 512
                    for k in range(N_DHT):
                        if stage == 0:
                            rhs = ABUF[:, k * TF + c0:k * TF + c0 + 512]
                            rr = rABl[k // 4]
                        else:
                            rhs = gsl(k, c0)
                            rr = rGBl[k // 4]
                        P.op("PE", lambda e, k=k, wb=wb, pb=pb, rhs=rhs: e.matmul(
                            pb[:, 0:512], wb[:, k * 128:(k + 1) * 128], rhs,
                            start=(k == 0), stop=(k == N_DHT - 1)),
                            reads=(rwb, rr), writes=(rpb,), tag=wnm + "mm")
                    if stage == 0:
                        P.op("ACT", lambda e, ob=ob, c0=c0, pb=pb: e.activation(
                            gsl(ob, c0), pb[:, 0:512],
                            AF.Gelu, scale=FSC["sh1"][:, ob:ob + 1]),
                            reads=(rpb, rCONST),
                            writes=rRESall + (rGBl[ob // 4],), tag="gev")
                    else:
                        P.op("ACT", lambda e, ob=ob, c0=c0, pb=pb: e.activation(
                            ABUF[:, ob * TF + c0:ob * TF + c0 + 512], pb[:, 0:512],
                            AF.Gelu, scale=FSC["sh2"][:, ob:ob + 1]),
                            reads=(rpb, rCONST),
                            writes=(rAB, rABl[ob // 4]), tag="g2ev")
        for ob in range(N_DMT):
            wb, rwb = WH[ob % 2], rWH[ob % 2]
            wblk_load(wb, rwb, dr["w3"], ob * DH * 128, DH, 128,
                      extra_reads=(rUSP, rBAR), tag="w3")

            for nh in range(2):
                pb, rpb = (PSB[0], rPS[0]) if nh == 0 else (PSB[1], rPS[1])
                ev, rev = (YSC, rYS) if nh == 0 else (None, rMS)
                c0 = nh * 512
                P.dma("SP", "uld", lambda e, ob=ob, c0=c0, ct0=ct0:
                      e.dma_start(ULD[:, 0:512],
                                  bass.AP(uspill, ob * 128 * T + ct0 + c0,
                                          [[T, 128], [1, 512]])),
                      reads=(rUSP,), writes=(rULD,), tag="x1ld")
                for k in range(N_DHT):
                    P.op("PE", lambda e, k=k, c0=c0, wb=wb, pb=pb: e.matmul(
                        pb[:, 0:512], wb[:, k * 128:(k + 1) * 128],
                        ABUF[:, k * TF + c0:k * TF + c0 + 512],
                        start=(k == 0), stop=(k == N_DHT - 1)),
                        reads=(rwb, rABl[k // 4]), writes=(rpb,), tag="w3mm")
                evap = (YSC[:, 0:512] if nh == 0
                        else ESCR[:, :].bitcast(F32)[:, 0:512])
                P.op("DVE", lambda e, ob=ob, c0=c0, pb=pb, evap=evap:
                     e.scalar_tensor_tensor(
                         evap, pb[:, 0:512], FSC["s3"][:, ob:ob + 1],
                         ULD[:, 0:512],
                         ALU.mult, ALU.add),
                     reads=(rpb, rULD, rCONST),
                     writes=((rYS, rYSl[0], rYSl[1], rYSl[2], rYSl[3])
                             if nh == 0 else (rMS, rMS2, rES)), tag="w3ev")
                P.dma("SP", "out", lambda e, ob=ob, c0=c0, ct0=ct0, evap=evap:
                      e.dma_start(bass.AP(y_out, ob * 128 * T + ct0 + c0,
                                          [[T, 128], [1, 512]]), evap),
                      reads=((rYS,) if nh == 0 else (rMS,)), writes=(rOUT,),
                      tag="ystore")

    P.finalize()
    return nc, P



def emit(nc, P):
    from contextlib import ExitStack
    st = ExitStack()
    sems = {}
    for e in Prog.ENGS:
        sems[("E", e)] = st.enter_context(nc.semaphore("se_" + e))
    for s in P.dma_sems:
        sems[("D", s)] = st.enter_context(nc.semaphore("sd_" + s))
    block = st.enter_context(nc.Block())

    by_eng = {e: [] for e in Prog.ENGS}
    for o, w in zip(P.ops, P.waits):
        by_eng[o.eng].append((o, w))

    def run_stream(eng_obj, eng_name):
        for o, w in by_eng[eng_name]:
            for key, val in w.items():
                eng_obj.wait_ge(sems[key], val)
            inst = o.fn(eng_obj)
            if o.is_dma:
                inst.then_inc(sems[("D", o.sem)], 16)
            elif o.need_inc:
                inst.then_inc(sems[("E", o.eng)], 1)
        if eng_name == "SP":
            eng_obj.wait_ge(sems[("D", "out")], P.dma_sems["out"])
            if "dbgout" in P.dma_sems:
                eng_obj.wait_ge(sems[("D", "dbgout")], P.dma_sems["dbgout"])

    @block.tensor
    def _(e):
        run_stream(e, "PE")

    @block.vector
    def _(e):
        run_stream(e, "DVE")

    @block.scalar
    def _(e):
        run_stream(e, "ACT")

    @block.gpsimd
    def _(e):
        run_stream(e, "POOL")

    @block.sync
    def _(e):
        run_stream(e, "SP")

    st.close()


_CACHE = {}


def _get_program(debug_taps=()):
    key = tuple(debug_taps)
    if key not in _CACHE:
        nc, P = build_program(debug_taps)
        emit(nc, P)
        _CACHE[key] = nc
    return _CACHE[key]


def kernel(**inputs):
    per_core = _prep_inputs(inputs)
    nc = _get_program()
    res = run_bass_kernel_spmd(nc, per_core, core_ids=list(range(NCORES)))
    out = np.stack([np.ascontiguousarray(np.asarray(r["y"]).T)
                    for r in res.results])
    return out.astype(np.float32)


# version 20
# speedup vs baseline: 1.2270x; 1.2270x over previous
import numpy as np
import ml_dtypes
import concourse.bass as bass
import concourse.mybir as mybir
from concourse.bass_utils import run_bass_kernel_spmd

F32 = mybir.dt.float32
BF16 = mybir.dt.bfloat16
AF = mybir.ActivationFunctionType
ALU = mybir.AluOpType

DM = 1024
DI = 2048
H = 32
PH = 64
NS = 64
DC = 4
CS = 128
T = 2048
TQ = 512
NQ = T // TQ
CHQ = TQ // CS
DIN = 2 * DI + 2 * NS + H
DH = 4096
EPS = 1e-5
NCORES = 8
N_DMT = DM // 128
N_DHT = DH // 128
XW = 4 + TQ
TF = 1024

INP_BLOCKS = [128] * 16 + [128] * 16 + [64, 64, 32]



class Res:
    __slots__ = ("name", "writer", "readers")

    def __init__(self, name):
        self.name = name
        self.writer = None
        self.readers = []


class Op:
    __slots__ = ("idx", "eng", "fn", "deps", "is_dma", "sem", "semval",
                 "need_inc", "know", "tag")

    def __init__(self, idx, eng, fn, is_dma, tag):
        self.idx = idx
        self.eng = eng
        self.fn = fn
        self.deps = []
        self.is_dma = is_dma
        self.sem = None
        self.semval = None
        self.need_inc = False
        self.know = None
        self.tag = tag


class Prog:
    ENGS = ("PE", "DVE", "ACT", "POOL", "SP")

    def __init__(self):
        self.ops = []
        self.dma_sems = {}

    def op(self, eng, fn, reads=(), writes=(), tag=""):
        o = Op(len(self.ops), eng, fn, False, tag)
        self._wire(o, reads, writes)
        self.ops.append(o)
        return o

    def dma(self, queue, sem_name, fn, reads=(), writes=(), tag=""):
        o = Op(len(self.ops), queue, fn, True, tag)
        o.sem = sem_name
        self.dma_sems[sem_name] = self.dma_sems.get(sem_name, 0) + 16
        o.semval = self.dma_sems[sem_name]
        self._wire(o, reads, writes)
        self.ops.append(o)
        return o

    def _wire(self, o, reads, writes):
        dep = set()
        for r in reads:
            if r.writer is not None:
                dep.add(r.writer)
        for w in writes:
            if w.writer is not None:
                dep.add(w.writer)
            for rd in w.readers:
                dep.add(rd)
        dep.discard(o)
        o.deps = sorted(dep, key=lambda x: x.idx)
        for r in reads:
            r.readers.append(o)
        for w in writes:
            w.writer = o
            w.readers = []

    def finalize(self):
        def needs_edge(d, o):
            if d.is_dma:
                return True
            if d.eng == o.eng and d.eng == "PE":
                return False
            return True

        for o in self.ops:
            for d in o.deps:
                if (not d.is_dma) and needs_edge(d, o):
                    d.need_inc = True
        count = {e: 0 for e in self.ENGS}
        for o in self.ops:
            if (not o.is_dma) and o.need_inc:
                count[o.eng] += 1
                o.semval = count[o.eng]
        self.final_counts = dict(count)
        know = {e: {} for e in self.ENGS}
        waits = []
        for o in self.ops:
            k = know[o.eng]
            w = {}
            for d in o.deps:
                if not needs_edge(d, o):
                    continue
                if d.is_dma:
                    key, val = ("D", d.sem), d.semval
                else:
                    key, val = ("E", d.eng), d.semval
                    if val is None:
                        continue
                if k.get(key, 0) >= val:
                    continue
                if w.get(key, 0) < val:
                    w[key] = val
                for kk, vv in d.know.items():
                    if k.get(kk, 0) < vv:
                        k[kk] = vv
                k[key] = max(k.get(key, 0), val)
            waits.append(w)
            ksnap = dict(k)
            if o.is_dma:
                ksnap[("D", o.sem)] = o.semval
            elif o.need_inc:
                ksnap[("E", o.eng)] = o.semval
            o.know = ksnap
        self.waits = waits



def _blocked_wT(w, widths=None):
    wt = np.ascontiguousarray(np.asarray(w).T)
    if widths is None:
        widths = [128] * (np.asarray(w).shape[0] // 128)
    parts, o = [], 0
    for ow in widths:
        parts.append(np.ascontiguousarray(wt[:, o:o + ow]).reshape(-1))
        o += ow
    return np.concatenate(parts)


def _prep_inputs(inputs):
    g = {}
    for i in range(2):
        g[f"inw{i}"] = _blocked_wT(inputs["L0_attn_in_w"][i], INP_BLOCKS)
        g[f"outw{i}"] = _blocked_wT(inputs["L0_attn_out_w"][i])
        cw = np.asarray(inputs["L0_attn_conv_w"][i])
        g[f"convwx{i}"] = np.ascontiguousarray(
            cw[:DI].reshape(16, 128, DC).transpose(1, 0, 2).reshape(128, 16 * DC))
        g[f"convwbc{i}"] = np.ascontiguousarray(
            np.concatenate([cw[DI:DI + NS], cw[DI + NS:]], axis=1))
        cb = np.asarray(inputs["L0_attn_conv_b"][i])
        g[f"convbx{i}"] = np.ascontiguousarray(cb[:DI].reshape(16, 128).T)
        g[f"convbbc{i}"] = np.ascontiguousarray(
            np.stack([cb[DI:DI + NS], cb[DI + NS:]], axis=1))
        g[f"dtb{i}"] = np.ascontiguousarray(inputs["L0_attn_dt_bias"][i]).reshape(H, 1)
        g[f"alog{i}"] = np.ascontiguousarray(inputs["L0_attn_A_log"][i]).reshape(H, 1)
        g[f"drep{i}"] = np.ascontiguousarray(
            np.repeat(np.asarray(inputs["L0_attn_Dp"][i]), PH).reshape(16, 128).T)
        g[f"rmsw{i}"] = np.ascontiguousarray(
            np.asarray(inputs["L0_attn_rms_w"][i]).reshape(16, 128).T)
        g[f"normw{i}"] = np.ascontiguousarray(
            np.asarray(inputs["L0_attn_norm_w"][i]).reshape(8, 128).T)
    for nm, key in (("w1", "L0_ffn_w1_w"), ("w2", "L0_ffn_w2_w"),
                    ("h1", "L0_ffn_h1_w"), ("h2", "L0_ffn_h2_w"),
                    ("w3", "L0_ffn_w3_w")):
        g[nm] = _blocked_wT(inputs[key])
    for nm, key in (("s1", "L0_ffn_w1_s"), ("s2", "L0_ffn_w2_s"),
                    ("sh1", "L0_ffn_h1_s"), ("sh2", "L0_ffn_h2_s"),
                    ("s3", "L0_ffn_w3_s")):
        g[nm] = np.ascontiguousarray(np.asarray(inputs[key]).reshape(-1, 128).T)
    g["epsc"] = np.full((1, 1), EPS, dtype=np.float32)
    g["identb"] = np.eye(128, dtype=ml_dtypes.bfloat16)
    g["identf"] = np.eye(128, dtype=np.float32)
    g["onesb"] = np.ones((128, 128), dtype=ml_dtypes.bfloat16)
    m01 = np.zeros((128, 128), dtype=ml_dtypes.bfloat16)
    for j in range(128):
        m01[j, j:] = 1
    g["mask01"] = m01
    sg = np.zeros((H, 4, 512), dtype=ml_dtypes.bfloat16)
    for q in range(4):
        for hh in range(8):
            sg[q * 8 + hh, q, hh * 64:(hh + 1) * 64] = 1
    g["sgrp"] = np.ascontiguousarray(sg.reshape(H, 4 * 512))

    u = np.asarray(inputs["L0_attn_u"])
    per_core = []
    for c in range(NCORES):
        d = dict(g)
        d["u"] = np.ascontiguousarray(u[c].T)
        per_core.append(d)
    return per_core



def build_program(debug_taps=()):
    nc = bass.Bass("TRN2", target_bir_lowering=False)
    P = Prog()

    dr = {}
    def din(name, shape, dt=F32):
        dr[name] = nc.dram_tensor(name, shape, dt, kind="ExternalInput")

    din("u", [DM, T])
    for i in range(2):
        din(f"inw{i}", [DM * DIN]); din(f"outw{i}", [DI * DM])
        din(f"convwx{i}", [128, 16 * DC]); din(f"convwbc{i}", [NS, 2 * DC])
        din(f"convbx{i}", [128, 16]); din(f"convbbc{i}", [NS, 2])
        din(f"dtb{i}", [H, 1]); din(f"alog{i}", [H, 1])
        din(f"drep{i}", [128, 16]); din(f"rmsw{i}", [128, 16]); din(f"normw{i}", [128, 8])
    din("w1", [DM * DH]); din("w2", [DM * DH])
    din("h1", [DH * DH]); din("h2", [DH * DH]); din("w3", [DH * DM])
    for nm in ("s1", "s2", "sh1", "sh2"):
        din(nm, [128, DH // 128])
    din("s3", [128, DM // 128])
    din("epsc", [1, 1])
    din("identb", [128, 128], BF16); din("identf", [128, 128])
    din("onesb", [128, 128], BF16); din("mask01", [128, 128], BF16)
    din("sgrp", [H, 4 * 512], BF16)

    y_out = nc.dram_tensor("y", [DM, T], F32, kind="ExternalOutput")
    uspill = nc.dram_tensor("uspill", [DM, T], F32)
    dbg = {}
    for tname, shape in debug_taps:
        dbg[tname] = nc.dram_tensor("dbg_" + tname, list(shape), F32,
                                    kind="ExternalOutput")

    live_cms = {}
    def sbuf(name, shape, dt, side="left"):
        cm = nc.sbuf_tensor(name, shape, dt, side=side)
        h = cm.__enter__()
        live_cms[name] = cm
        return h

    def sbuf_free(name):
        live_cms.pop(name).__exit__(None, None, None)

    RESID = sbuf("RESID", [128, N_DMT * T], F32)
    STAT = sbuf("STAT", [1, 1024], F32)
    ULD = sbuf("ULD", [128, 512], F32)
    YSC = sbuf("YSC", [128, 512], F32)
    ESCR = sbuf("ESCR", [128, 1024], BF16)
    SQT = sbuf("SQT", [128, 512], BF16)
    WA = sbuf("WA", [128, 16 * 128], BF16)
    WB = sbuf("WB", [128, 16 * 128], BF16)
    WC = sbuf("WC", [128, 16 * 128], BF16)
    EPSC = sbuf("EPSC", [1, 1], F32)
    IDB = sbuf("IDB", [128, 128], BF16)
    IDF = sbuf("IDF", [128, 128], F32)
    ONESB = sbuf("ONESB", [128, 128], BF16)
    MASK01 = sbuf("MASK01", [128, 128], BF16)
    SGRP = sbuf("SGRP", [H, 4 * 512], BF16)
    ONES32F = sbuf("ONES32F", [32, CS], F32)
    FSC = {nm: sbuf("FSC_" + nm, [128, (DH if nm != "s3" else DM) // 128], F32)
           for nm in ("s1", "s2", "sh1", "sh2", "s3")}
    CWX = [sbuf(f"CWX{i}", [128, 16 * DC], F32) for i in range(2)]
    CWBC = [sbuf(f"CWBC{i}", [64, 2 * DC], F32) for i in range(2)]
    CBX = [sbuf(f"CBX{i}", [128, 16], F32) for i in range(2)]
    CBBC = [sbuf(f"CBBC{i}", [64, 2], F32) for i in range(2)]
    DTB = [sbuf(f"DTB{i}", [H, 1], F32) for i in range(2)]
    ALOG = [sbuf(f"ALOG{i}", [H, 1], F32) for i in range(2)]
    AEXP = [sbuf(f"AEXP{i}", [H, 1], F32) for i in range(2)]
    DCOL = [sbuf(f"DCOL{i}", [128, 16], F32) for i in range(2)]
    RMSW = [sbuf(f"RMSW{i}", [128, 16], F32) for i in range(2)]
    NORMW = [sbuf(f"NORMW{i}", [128, 8], F32) for i in range(2)]

    SZ = sbuf("SZ", [128, 16 * TQ], BF16, side="right")
    XBC = sbuf("XBC", [128, 16 * TQ], BF16, side="right")
    ARN = sbuf("ARN", [128, 16 * XW], BF16, side="right")
    NRM = sbuf("NRM", [128, N_DMT * TQ], BF16, side="right")
    DIAG = sbuf("DIAG", [128, 64 * 128], BF16, side="right")
    BCRAW = sbuf("BCRAW", [64, 2 * XW], BF16, side="right")
    BCC = sbuf("BCC", [64, 2 * TQ], BF16, side="right")
    DGBC = sbuf("DGBC", [64, 8 * 64], BF16, side="right")
    DSTACK = sbuf("DSTACK", [128, TQ], F32, side="right")
    TMC = sbuf("TMC", [128, 4 * 192], F32, side="right")
    XDT = sbuf("XDT", [128, DI], BF16, side="right")
    XDD = sbuf("XDD", [128, DI], BF16, side="right")
    YTM = sbuf("YTM", [128, DI], BF16, side="right")
    GM = sbuf("GM", [128, 3 * 512], BF16, side="right")
    GTM = sbuf("GTM", [128, 4 * 128], BF16, side="right")
    BTM = sbuf("BTM", [128, 4 * NS], BF16, side="right")
    PREV = sbuf("PREV", [64, H * PH], F32, side="right")
    PREVB = sbuf("PREVB", [64, H * PH], BF16, side="right")
    CDR = sbuf("CDR", [32, 4], BF16, side="right")
    CDROW = sbuf("CDROW", [1, 2 * 512], BF16, side="right")

    PSB = [nc.psum_tensor(f"PSB{i}", [128, 512], F32).__enter__() for i in range(6)]
    PBB = [nc.psum_tensor(f"PBB{i}", [128, 1024], BF16).__enter__() for i in range(2)]

    R = {}
    def res(name):
        if name not in R:
            R[name] = Res(name)
        return R[name]

    rRES = res("RESID")
    rRQ = [res(f"RESQ{i}") for i in range(4)]
    rRESall = (rRES, rRQ[0], rRQ[1], rRQ[2], rRQ[3])
    rSZ = res("SZ"); rXBC = res("XBC"); rARN = res("ARN")
    rNRM = res("NRM"); rDIAG = res("DIAG"); rBCRAW = res("BCRAW"); rBCC = res("BCC")
    rDGBC = res("DGBC"); rDST = res("DSTACK"); rDA = res("DAROW"); rTMC = res("TMC")
    rXDT = res("XDT"); rXDD = res("XDD"); rYTM = res("YTM")
    rGM = [res("GM0"), res("GM1"), res("GM2")]
    rARNl = [res(f"ARN{i}") for i in range(16)]
    rNRMl = [res(f"NRM{i}") for i in range(8)]
    rARNall = (rARN,) + tuple(rARNl)
    rTMCl = [res(f"TMC{i}") for i in range(4)]
    rGTMl = [res(f"GTM{i}") for i in range(4)]
    rBTMl = [res(f"BTM{i}") for i in range(4)]
    rCDRl = [res(f"CDR{i}") for i in range(4)]
    rCDROW2 = res("CDROW2"); rMS2 = res("MSCR2"); rMS3 = res("MSCR3")
    rGTM = res("GTM"); rBTM = res("BTM"); rPREV = res("PREV"); rPREVB = res("PREVB")
    rMS = res("MSCR"); rES = res("ESCR"); rYS = res("YSC"); rCDR = res("CDR")
    rYSl = [res(f"YSC{i}") for i in range(4)]
    rCDROW = res("CDROW"); rSTAT = res("STAT"); rSQT = res("SQT"); rULD = res("ULD")
    rWA = res("WA"); rWB = res("WB"); rWC = res("WC")
    rPS = [res(f"PSB{i}") for i in range(6)]
    rPB = [res(f"PBB{i}") for i in range(2)]
    rCONST = res("CONST")
    rUSP = res("USPILL")
    rOUT = res("OUT")

    def setup_load(dst_ap, src_ap):
        P.dma("SP", "set", lambda e, d=dst_ap, s=src_ap: e.dma_start(d, s),
              reads=(), writes=(rCONST,), tag="setup")

    setup_load(EPSC[:, :], dr["epsc"].ap())
    setup_load(IDB[:, :], dr["identb"].ap())
    setup_load(IDF[:, :], dr["identf"].ap())
    setup_load(ONESB[:, :], dr["onesb"].ap())
    setup_load(MASK01[:, :], dr["mask01"].ap())
    setup_load(SGRP[:, :], dr["sgrp"].ap())
    for i in range(2):
        setup_load(CWX[i][:, :], dr[f"convwx{i}"].ap())
        setup_load(CWBC[i][:, :], dr[f"convwbc{i}"].ap())
        setup_load(CBX[i][:, :], dr[f"convbx{i}"].ap())
        setup_load(CBBC[i][:, :], dr[f"convbbc{i}"].ap())
        setup_load(DTB[i][:, :], dr[f"dtb{i}"].ap())
        setup_load(ALOG[i][:, :], dr[f"alog{i}"].ap())
        setup_load(DCOL[i][:, :], dr[f"drep{i}"].ap())
        setup_load(RMSW[i][:, :], dr[f"rmsw{i}"].ap())
        setup_load(NORMW[i][:, :], dr[f"normw{i}"].ap())
    for nm in ("s1", "s2", "sh1", "sh2", "s3"):
        setup_load(FSC[nm][:, :], dr[nm].ap())
    P.dma("SP", "set", lambda e: e.dma_start(
        RESID[:, :].rearrange("p (a t) -> p a t", t=T),
        dr["u"].ap().rearrange("(a p) t -> p a t", p=128)),
        reads=(), writes=rRESall + (rCONST,), tag="uload")

    P.op("DVE", lambda e: e.memset(ONES32F[:, :], 1.0),
         reads=(rCONST,), writes=(rCONST,), tag="ones32")
    for i in range(2):
        P.op("ACT", lambda e, i=i: e.activation(AEXP[i][:, :], ALOG[i][:, :], AF.Exp),
             reads=(rCONST,), writes=(rCONST,), tag="aexp")

    def dbg_store(name, sb_ap, reads):
        if name in dbg:
            P.dma("POOL", "dbgout", lambda e, d=dbg[name], s=sb_ap: e.dma_start(d.ap(), s),
                  reads=tuple(reads), writes=(rOUT,), tag="dbg")

    def wblk_load(wb, rwb, dram, elem_off, K, ow, extra_reads=(), tag="wl"):
        kt = K // 128
        src = bass.AP(dram, elem_off, [[ow, 128], [128 * ow, kt], [1, ow]])
        dst = wb[:, 0:kt * ow].rearrange("p (k o) -> p k o", o=ow)
        P.dma("POOL", "wb_" + rwb.name,
              lambda e, d=dst, s=src: e.dma_start(d, s),
              reads=tuple(extra_reads), writes=(rwb,), tag=tag)

    def rank1_bcast(row_ap, bank, reads, tag):
        P.op("PE", lambda e, r=row_ap: e.matmul(
            PSB[bank][:, 0:512], ONESB[0:1, :], r, start=True, stop=True),
            reads=tuple(reads) + (rCONST,), writes=(rPS[bank],), tag=tag)

    STATB = YSC[0:1, 0:512].bitcast(BF16)

    def rownorm_istd(nelem, tag):
        P.op("ACT", lambda e: e.activation(
            STAT[0:1, 0:512], STAT[0:1, 0:512], AF.Ln, bias=EPSC[0:1, 0:1],
            scale=1.0 / nelem),
            reads=(rSTAT, rCONST), writes=(rSTAT,), tag=tag + "ln")
        P.op("ACT", lambda e: e.activation(
            STATB[0:1, 512:1024], STAT[0:1, 0:512], AF.Exp, scale=-0.5),
            reads=(rSTAT,),
            writes=(rSTAT, rYS, rYSl[0], rYSl[1], rYSl[2], rYSl[3]),
            tag=tag + "exp")

    def layernorm(dst_fn, tag):
        for tp in range(T // 512):
            t0 = tp * 512
            for k in range(N_DMT):
                P.op("DVE", lambda e, k=k, t0=t0: e.tensor_tensor(
                    SQT[:, 0:512], RESID[:, k * T + t0:k * T + t0 + 512],
                    RESID[:, k * T + t0:k * T + t0 + 512], ALU.mult),
                    reads=(rRQ[tp],), writes=(rSQT,), tag=tag + "sq")
                P.op("DVE", lambda e, k=k, t0=t0: e.tensor_copy(
                    ESCR[:, 0:512], RESID[:, k * T + t0:k * T + t0 + 512]),
                    reads=(rRQ[tp],), writes=(rES, rMS, rMS2), tag=tag + "cp")
                P.op("PE", lambda e, k=k: e.matmul(
                    PSB[2][0:1, 0:512], ONESB[:, 0:1], SQT[:, 0:512],
                    start=(k == 0), stop=(k == N_DMT - 1)),
                    reads=(rSQT, rCONST), writes=(rPS[2],), tag=tag + "mmsq")
                P.op("PE", lambda e, k=k: e.matmul(
                    PSB[5][0:1, 0:512], ONESB[:, 0:1], ESCR[:, 0:512],
                    start=(k == 0), stop=(k == N_DMT - 1)),
                    reads=(rES, rCONST), writes=(rPS[5],), tag=tag + "mmsum")
            P.op("DVE", lambda e: e.tensor_copy(STAT[0:1, 0:512], PSB[2][0:1, 0:512]),
                 reads=(rPS[2],), writes=(rSTAT,), tag=tag + "evA")
            P.op("DVE", lambda e: e.tensor_scalar(
                STAT[0:1, 512:1024], PSB[5][0:1, 0:512], 1.0 / DM, None, ALU.mult),
                reads=(rPS[5],), writes=(rSTAT,), tag=tag + "mean")
            P.op("DVE", lambda e: e.tensor_tensor(
                YSC[0:1, 0:512], STAT[0:1, 512:1024], STAT[0:1, 512:1024],
                ALU.mult),
                reads=(rSTAT,), writes=(rYS,), tag=tag + "m2")
            P.op("DVE", lambda e: e.scalar_tensor_tensor(
                STAT[0:1, 0:512], STAT[0:1, 0:512], 1.0 / DM,
                YSC[0:1, 0:512], ALU.mult, ALU.subtract),
                reads=(rSTAT, rYS), writes=(rSTAT,), tag=tag + "var")
            P.op("DVE", lambda e: e.tensor_copy(
                STATB[0:1, 0:512], STAT[0:1, 512:1024]),
                reads=(rSTAT,),
                writes=(rSTAT, rYS, rYSl[0], rYSl[1], rYSl[2], rYSl[3]),
                tag=tag + "meanb")
            rownorm_istd(1, tag)
            rank1_bcast(STATB[0:1, 512:1024], 3, (rSTAT, rYS), tag + "bi")
            rank1_bcast(STATB[0:1, 0:512], 4, (rSTAT, rYS), tag + "bm")
            for k in range(N_DMT):
                P.op("DVE", lambda e, k=k, t0=t0: e.tensor_tensor(
                    ULD[:, 0:512], RESID[:, k * T + t0:k * T + t0 + 512],
                    PSB[4][:, 0:512], ALU.subtract),
                    reads=(rRQ[tp], rPS[4]), writes=(rULD,), tag=tag + "sub")
                dst_fn(k, t0, tp)

    def attn_layer(li):
        inw = dr[f"inw{li}"]; outw = dr[f"outw{li}"]
        P.op("DVE", lambda e: e.memset(PREV[0:64, :], 0.0),
             reads=(), writes=(rPREV,), tag="prev0")
        P.op("DVE", lambda e: e.memset(PREVB[0:64, :], 0.0),
             reads=(), writes=(rPREVB,), tag="prevb0")
        for t in range(16):
            for w in range(DC):
                P.op("ACT", lambda e, t=t, w=w: e.activation(
                    DIAG[:, (t * DC + w) * 128:(t * DC + w + 1) * 128],
                    IDB[:, :], AF.Copy,
                    scale=CWX[li][:, t * DC + w:t * DC + w + 1]),
                    reads=(rCONST,), writes=(rDIAG,), tag="diagx")
        for bc in range(2):
            for w in range(DC):
                P.op("ACT", lambda e, bc=bc, w=w: e.activation(
                    DGBC[0:64, (bc * DC + w) * 64:(bc * DC + w + 1) * 64],
                    IDB[0:64, 0:64], AF.Copy,
                    scale=CWBC[li][0:64, bc * DC + w:bc * DC + w + 1]),
                    reads=(rCONST,), writes=(rDGBC,), tag="diagbc")

        blk_off = [0]
        for ow in INP_BLOCKS:
            blk_off.append(blk_off[-1] + DM * ow)

        for qi in range(NQ):
            t0 = qi * TQ
            for k in range(N_DMT):
                P.op("DVE", lambda e, k=k, t0=t0: e.tensor_tensor(
                    SQT[:, 0:512], RESID[:, k * T + t0:k * T + t0 + 512],
                    RESID[:, k * T + t0:k * T + t0 + 512], ALU.mult),
                    reads=(rRQ[qi],), writes=(rSQT,), tag="rmsq")
                P.op("PE", lambda e, k=k: e.matmul(
                    PSB[2][0:1, 0:512], ONESB[:, 0:1], SQT[:, 0:512],
                    start=(k == 0), stop=(k == N_DMT - 1)),
                    reads=(rSQT, rCONST), writes=(rPS[2],), tag="rmsmm")
            P.op("DVE", lambda e: e.tensor_copy(STAT[0:1, 0:512], PSB[2][0:1, 0:512]),
                 reads=(rPS[2],), writes=(rSTAT,), tag="rmsev")
            rownorm_istd(DM, "rms")
            rank1_bcast(STATB[0:1, 512:1024], 3, (rSTAT, rYS), "rmsbc")
            for k in range(N_DMT):
                P.op("DVE", lambda e, k=k, t0=t0: e.scalar_tensor_tensor(
                    NRM[:, k * TQ:(k + 1) * TQ],
                    RESID[:, k * T + t0:k * T + t0 + 512],
                    NORMW[li][:, k:k + 1], PSB[3][:, 0:512],
                    ALU.mult, ALU.mult),
                    reads=(rRQ[qi], rPS[3], rCONST),
                    writes=(rNRM, rNRMl[k]), tag="rmsap")
            dbg_store(f"normed{li}_{qi}", NRM[:, :], (rNRM,))

            if qi == 0:
                P.op("DVE", lambda e: e.memset(ARN[:, :], 0.0),
                     reads=(rARN,), writes=rARNall, tag="tail0")
                P.op("DVE", lambda e: e.memset(BCRAW[0:64, :], 0.0),
                     reads=(rBCRAW,), writes=(rBCRAW,), tag="tail0bc")

            groups = [(34,), (32, 33)] + [(2 * i, 2 * i + 1) for i in range(16)]
            wrot = [(WA, rWA), (WB, rWB), (WC, rWC)]

            def emit_inproj_group(gi, grp):
                wb, rwb = wrot[gi % 3]
                nb = len(grp)
                ow = INP_BLOCKS[grp[0]]
                src = bass.AP(inw, blk_off[grp[0]],
                              [[ow, 128], [DM * ow, nb], [128 * ow, DM // 128],
                               [1, ow]])
                dst = wb[:, 0:8 * nb * ow].rearrange("p (b k o) -> p b k o",
                                                     b=nb, o=ow)
                P.dma("POOL", "wb_" + rwb.name,
                      lambda e, d=dst, s=src: e.dma_start(d, s),
                      reads=(), writes=(rwb,), tag=f"inwg{gi}")
                for bi, ob in enumerate(grp):
                    pb, rpb = (PSB[0], rPS[0]) if ob % 2 == 0 else (PSB[1], rPS[1])
                    for k in range(N_DMT):
                        P.op("PE", lambda e, ow=ow, k=k, bi=bi, wb=wb, pb=pb:
                             e.matmul(pb[0:ow, 0:512],
                                      wb[:, (bi * 8 + k) * ow:(bi * 8 + k + 1) * ow],
                                      NRM[:, k * TQ:k * TQ + 512],
                                      start=(k == 0), stop=(k == N_DMT - 1)),
                             reads=(rwb, rNRMl[k]), writes=(rpb,), tag="inpmm")
                    if ob < 16:
                        P.op("ACT", lambda e, ob=ob, pb=pb: e.activation(
                            SZ[:, ob * TQ:(ob + 1) * TQ], pb[:, 0:512], AF.Silu),
                            reads=(rpb,), writes=(rSZ,), tag="zev")
                    elif ob < 32:
                        t = ob - 16
                        P.op("DVE", lambda e, t=t, pb=pb: e.tensor_copy(
                            ARN[:, t * XW + 4:t * XW + 4 + 512], pb[:, 0:512]),
                            reads=(rpb,), writes=(rARN, rARNl[t]), tag="xev")
                    elif ob < 34:
                        bc = ob - 32
                        P.op("DVE", lambda e, bc=bc, pb=pb: e.tensor_copy(
                            BCRAW[0:64, bc * XW + 4:bc * XW + 4 + 512],
                            pb[0:64, 0:512]),
                            reads=(rpb,), writes=(rBCRAW,), tag="bcev")
                    else:
                        P.op("DVE", lambda e, pb=pb: e.tensor_copy(
                            DSTACK[0:32, 0:512], pb[0:32, 0:512]),
                            reads=(rpb,), writes=(rDST,), tag="dtev")

            def emit_bcconv_dt():
                for bc in range(2):
                    pb, rpb = (PSB[4], rPS[4]) if bc == 0 else (PSB[5], rPS[5])
                    for w in range(DC):
                        P.op("PE", lambda e, bc=bc, w=w, pb=pb: e.matmul(
                            pb[0:64, 0:512],
                            DGBC[0:64, (bc * DC + w) * 64:(bc * DC + w + 1) * 64],
                            BCRAW[0:64, bc * XW + 1 + w:bc * XW + 1 + w + 512],
                            start=(w == 0), stop=(w == DC - 1)),
                            reads=(rDGBC, rBCRAW), writes=(rpb,), tag="convmmbc")
                    P.op("ACT", lambda e, bc=bc, pb=pb: e.activation(
                        BCC[0:64, bc * TQ:(bc + 1) * TQ], pb[0:64, 0:512],
                        AF.Silu, bias=CBBC[li][0:64, bc:bc + 1]),
                        reads=(rpb, rCONST), writes=(rBCC,), tag="convevbc")
                P.op("ACT", lambda e: e.activation(
                    YSC[0:32, :], DSTACK[0:32, :], AF.Abs, bias=DTB[li][:, :]),
                    reads=(rDST, rCONST),
                    writes=(rYS, rYSl[0], rYSl[1], rYSl[2], rYSl[3]),
                    tag="sp_abs")
                P.op("ACT", lambda e: e.activation(
                    YSC[0:32, :], YSC[0:32, :], AF.Exp, scale=-1.0),
                    reads=(rYS,), writes=(rYS,), tag="sp_exp")
                P.op("ACT", lambda e: e.activation(
                    YSC[0:32, :], YSC[0:32, :], AF.Ln, bias=1.0),
                    reads=(rYS,), writes=(rYS,), tag="sp_ln")
                P.op("ACT", lambda e: e.activation(
                    DSTACK[0:32, :], DSTACK[0:32, :], AF.Relu, bias=DTB[li][:, :]),
                    reads=(rDST, rCONST), writes=(rDST,), tag="sp_relu")
                P.op("DVE", lambda e: e.tensor_tensor(
                    DSTACK[0:32, :], DSTACK[0:32, :], YSC[0:32, :], ALU.add),
                    reads=(rDST, rYS), writes=(rDST,), tag="sp_add")
                P.op("DVE", lambda e: e.tensor_scalar(
                    YSC[0:32, :], DSTACK[0:32, :], AEXP[li][:, :], -1.0,
                    ALU.mult, ALU.mult),
                    reads=(rDST, rCONST), writes=(rYS,), tag="dA")

            def emit_front(c):
                c0, c1 = c * CS, (c + 1) * CS
                tmc = TMC[:, c * 192:c * 192 + 192]
                rtmc = rTMCl[c]
                gtm = GTM[:, c * 128:c * 128 + 128]
                rgtm = rGTMl[c]
                btm = BTM[:, c * NS:c * NS + NS]
                rbtm = rBTMl[c]
                cdr = CDR[0:32, c:c + 1]
                rcdr = rCDRl[c]
                P.op("DVE", lambda e, c0=c0, c1=c1: e.tensor_tensor_scan(
                    DSTACK[32:64, c0:c1], ONES32F[:, :], YSC[0:32, c0:c1],
                    0.0, ALU.mult, ALU.add),
                    reads=(rYS, rCONST), writes=(rDST,), tag="cumsum")
                P.op("ACT", lambda e, c0=c0, c1=c1: e.activation(
                    DSTACK[64:96, c0:c1], DSTACK[32:64, c0:c1], AF.Exp,
                    scale=-1.0, bias=DSTACK[32:64, c1 - 1:c1]),
                    reads=(rDST,), writes=(rDST,), tag="decay")
                P.op("ACT", lambda e, c0=c0, c1=c1: e.activation(
                    DSTACK[96:128, c0:c1], DSTACK[32:64, c0:c1], AF.Exp),
                    reads=(rDST,), writes=(rDST,), tag="ecum")
                P.op("PE", lambda e, c0=c0, c1=c1: e.transpose(
                    PSB[4][:, 0:128], DSTACK[:, c0:c1], IDF[:, :]),
                    reads=(rDST, rCONST), writes=(rPS[4],), tag="stackT")
                P.op("DVE", lambda e, tmc=tmc: e.tensor_copy(
                    tmc[:, 0:128], PSB[4][:, 0:128]),
                    reads=(rPS[4],), writes=(rtmc,), tag="stackEv")
                P.op("DVE", lambda e, tmc=tmc: e.tensor_tensor(
                    tmc[:, 128:160], tmc[:, 0:32], tmc[:, 64:96], ALU.mult),
                    reads=(rtmc,), writes=(rtmc,), tag="dtdecay")
                P.op("ACT", lambda e, tmc=tmc: e.activation(
                    tmc[:, 160:192], tmc[:, 0:32], AF.Ln),
                    reads=(rtmc,), writes=(rtmc,), tag="lndt")
                P.op("DVE", lambda e, tmc=tmc: e.tensor_tensor(
                    tmc[:, 160:192], tmc[:, 160:192], tmc[:, 32:64],
                    ALU.subtract),
                    reads=(rtmc,), writes=(rtmc,), tag="bias2")
                P.op("DVE", lambda e, c1=c1, cdr=cdr: e.tensor_copy(
                    cdr, DSTACK[96:128, c1 - 1:c1]),
                    reads=(rDST,), writes=(rcdr,), tag="cd")
                P.op("PE", lambda e, c0=c0, c1=c1: e.matmul(
                    PSB[5][:, 0:128], BCC[0:64, c0:c1],
                    BCC[0:64, TQ + c0:TQ + c1], start=True, stop=True),
                    reads=(rBCC,), writes=(rPS[5],), tag="gt")
                P.op("DVE", lambda e, gtm=gtm: e.tensor_tensor(
                    gtm, PSB[5][:, 0:128], MASK01[:, :], ALU.mult),
                    reads=(rPS[5], rCONST), writes=(rgtm,), tag="gtev")
                P.op("PE", lambda e, c0=c0, c1=c1: e.transpose(
                    PBB[1][:, 0:64], BCC[0:64, c0:c1], IDB[0:64, 0:64]),
                    reads=(rBCC, rCONST), writes=(rPB[1],), tag="btmT")
                P.op("DVE", lambda e, btm=btm: e.tensor_copy(btm, PBB[1][:, 0:64]),
                     reads=(rPB[1],), writes=(rbtm,), tag="btmEv")

            for gi, grp in enumerate(groups):
                emit_inproj_group(gi, grp)
                if gi == 1:
                    emit_bcconv_dt()
                if gi == 5:
                    for c in range(CHQ):
                        emit_front(c)

            for t in range(16):
                pb, rpb = (PSB[0], rPS[0]) if t % 2 == 0 else (PSB[1], rPS[1])
                for w in range(DC):
                    P.op("PE", lambda e, t=t, w=w, pb=pb: e.matmul(
                        pb[:, 0:512],
                        DIAG[:, (t * DC + w) * 128:(t * DC + w + 1) * 128],
                        ARN[:, t * XW + 1 + w:t * XW + 1 + w + 512],
                        start=(w == 0), stop=(w == DC - 1)),
                        reads=(rDIAG, rARNl[t]), writes=(rpb,), tag="convmm")
                P.op("ACT", lambda e, t=t, pb=pb: e.activation(
                    XBC[:, t * TQ:(t + 1) * TQ], pb[:, 0:512],
                    AF.Silu, bias=CBX[li][:, t:t + 1]),
                    reads=(rpb, rCONST), writes=(rXBC,), tag="convev")
            P.op("DVE", lambda e: e.tensor_copy(
                ARN[:, :].rearrange("p (t c) -> p t c", c=XW)[:, :, 1:4],
                ARN[:, :].rearrange("p (t c) -> p t c", c=XW)[:, :, XW - 3:XW]),
                reads=(rARN,), writes=rARNall, tag="tail")
            P.op("DVE", lambda e: e.tensor_copy(
                BCRAW[0:64, :].rearrange("p (t c) -> p t c", c=XW)[:, :, 1:4],
                BCRAW[0:64, :].rearrange("p (t c) -> p t c", c=XW)[:, :, XW - 3:XW]),
                reads=(rBCRAW,), writes=(rBCRAW,), tag="tailbc")
            dbg_store(f"convx{li}_{qi}", XBC[:, :], (rXBC,))

            for c in range(CHQ):
                c0, c1 = c * CS, (c + 1) * CS
                par = c % 2
                tmc = TMC[:, c * 192:c * 192 + 192]
                rtmc = rTMCl[c]
                gtm = GTM[:, c * 128:c * 128 + 128]
                rgtm = rGTMl[c]
                btm = BTM[:, c * NS:c * NS + NS]
                rbtm = rBTMl[c]
                cdr = CDR[0:32, c:c + 1]
                rcdr = rCDRl[c]
                cdrow = CDROW[0:1, par * 512:par * 512 + 512]
                rcdrow = rCDROW if par == 0 else rCDROW2
                for wv in range(2):
                    for dp in range(8):
                        d = wv * 8 + dp
                        P.op("PE", lambda e, d=d, dp=dp, c0=c0, c1=c1: e.transpose(
                            PBB[0][:, dp * 128:(dp + 1) * 128],
                            XBC[:, d * TQ + c0:d * TQ + c1], IDB[:, :]),
                            reads=(rXBC, rCONST), writes=(rPB[0],), tag="xT")
                    P.op("DVE", lambda e, wv=wv: e.tensor_copy(
                        XDT[:, wv * 1024:(wv + 1) * 1024], PBB[0][:, :]),
                        reads=(rPB[0],), writes=(rXDT,), tag="xev2")
                for sub in range(4):
                    P.op("DVE", lambda e, sub=sub, tmc=tmc: e.tensor_tensor(
                        XDD[:, sub * 512:(sub + 1) * 512].rearrange(
                            "p (h j) -> p h j", j=PH),
                        XDT[:, sub * 512:(sub + 1) * 512].rearrange(
                            "p (h j) -> p h j", j=PH),
                        tmc[:, 128 + sub * 8:136 + sub * 8].unsqueeze(2)
                        .broadcast_to([128, 8, PH]), ALU.mult),
                        reads=(rXDT, rtmc), writes=(rXDD,), tag="xdd")
                PBB1F = PBB[1][:, :].bitcast(F32)
                MB3 = [PSB[4], PSB[5], PBB1F]
                rMB3 = [rPS[4], rPS[5], rPB[1]]
                rMSl = [rMS, rMS2, rMS3]

                def mask_front(g):
                    mi = g % 3
                    mb, rmb = MB3[mi], rMB3[mi]
                    for h4 in range(4):
                        h = 4 * g + h4
                        P.op("PE", lambda e, h=h, h4=h4, mb=mb, tmc=tmc: e.transpose(
                            mb[:, h4 * 128:(h4 + 1) * 128],
                            tmc[:, 32 + h:32 + h + 1].broadcast_to([128, 128]),
                            IDF[:, :]),
                            reads=(rtmc, rCONST), writes=(rmb,), tag="mbc")

                def mask_mid(g):
                    mi = g % 3
                    mb, rmb = MB3[mi], rMB3[mi]
                    es = (ESCR[:, mi * 512:(mi + 1) * 512] if mi < 2
                          else SQT[:, 0:512])
                    res_ = rMSl[mi]
                    esw = (res_,) if mi < 2 else (res_, rSQT)
                    gmap = GM[:, mi * 512:(mi + 1) * 512]
                    rgm = rGM[mi]
                    P.op("DVE", lambda e, g=g, mb=mb, tmc=tmc: e.tensor_tensor(
                        mb[:, 0:512].rearrange("p (h j) -> p h j", j=128),
                        mb[:, 0:512].rearrange("p (h j) -> p h j", j=128),
                        tmc[:, 160 + 4 * g:164 + 4 * g].unsqueeze(2)
                        .broadcast_to([128, 4, 128]), ALU.add),
                        reads=(rmb, rtmc), writes=(rmb,), tag="md2a")
                    P.op("DVE", lambda e, mb=mb: e.tensor_scalar(
                        mb[:, 0:512], mb[:, 0:512], 30.0, None, ALU.min),
                        reads=(rmb,), writes=(rmb,), tag="md2b")
                    P.op("ACT", lambda e, mb=mb, es=es: e.activation(
                        es, mb[:, 0:512], AF.Exp),
                        reads=(rmb,), writes=esw, tag="mexp")
                    P.op("DVE", lambda e, es=es, gmap=gmap, gtm=gtm:
                         e.tensor_tensor(
                             gmap.rearrange("p (r n) -> p r n", r=4),
                             es.rearrange("p (r n) -> p r n", r=4),
                             gtm.unsqueeze(1).broadcast_to([128, 4, 128]),
                             ALU.mult),
                         reads=esw + (rgtm,), writes=(rgm,), tag="mgm")


                def yoff_grp(g):
                    mi = g % 2
                    yob, ryo = (PSB[2 + mi], rPS[2 + mi])
                    for h4 in range(4):
                        h = 4 * g + h4
                        P.op("PE", lambda e, h=h, h4=h4, c0=c0, c1=c1, yob=yob:
                             e.matmul(yob[:, h4 * 64:(h4 + 1) * 64],
                                      BCC[0:64, TQ + c0:TQ + c1],
                                      PREVB[0:64, h * PH:(h + 1) * PH],
                                      start=True, stop=True),
                             reads=(rBCC, rPREVB), writes=(ryo,), tag="yoff")

                def ycsc_grp(g):
                    mi = g % 2
                    yob, ryo = (PSB[2 + mi], rPS[2 + mi])
                    ysc = YSC[:, mi * 256:(mi + 1) * 256]
                    P.op("DVE", lambda e, g=g, yob=yob, ysc=ysc, tmc=tmc:
                         e.tensor_tensor(
                             ysc.rearrange("p (h j) -> p h j", j=PH),
                             yob[:, 0:256].rearrange("p (h j) -> p h j", j=PH),
                             tmc[:, 96 + 4 * g:100 + 4 * g].unsqueeze(2)
                             .broadcast_to([128, 4, PH]), ALU.mult),
                         reads=(ryo, rtmc), writes=(rYSl[mi],), tag="ycsc")


                def ydiag_grp(g):
                    mi = g % 3
                    gmap = GM[:, mi * 512:(mi + 1) * 512]
                    rgm = rGM[mi]
                    ydb, ryd = (PSB[g % 2], rPS[g % 2])
                    for h4 in range(4):
                        h = 4 * g + h4
                        P.op("PE", lambda e, h=h, h4=h4, gmap=gmap, ydb=ydb:
                             e.matmul(ydb[:, h4 * 64:(h4 + 1) * 64],
                                      gmap[:, h4 * 128:(h4 + 1) * 128],
                                      XDT[:, h * PH:(h + 1) * PH],
                                      start=True, stop=True),
                             reads=(rgm, rXDT), writes=(ryd,), tag="ydiag")

                def ycomb_grp(g):
                    mi = g % 2
                    ydb, ryd = (PSB[g % 2], rPS[g % 2])
                    ysc = YSC[:, mi * 256:(mi + 1) * 256]
                    P.op("DVE", lambda e, g=g, ydb=ydb, ysc=ysc: e.tensor_tensor(
                        YTM[:, g * 256:(g + 1) * 256], ydb[:, 0:256], ysc,
                        ALU.add),
                        reads=(ryd, rYSl[mi]), writes=(rYTM,), tag="ycomb")

                mask_front(0); mask_front(1); mask_front(2)
                mask_mid(0); mask_mid(1)
                yoff_grp(0)
                ycsc_grp(0)
                for g in range(8):
                    ydiag_grp(g)
                    if g + 3 < 8:
                        mask_front(g + 3)
                    if g + 2 < 8:
                        mask_mid(g + 2)
                    if g + 1 < 8:
                        yoff_grp(g + 1)
                        ycsc_grp(g + 1)
                    ycomb_grp(g)
                for q in range(4):
                    sb_, rsb = (PSB[4 + q % 2], rPS[4 + q % 2])
                    cb_, rcb = (PSB[4 + (q + 1) % 2], rPS[4 + (q + 1) % 2])
                    for hh in range(8):
                        h = q * 8 + hh
                        P.op("PE", lambda e, h=h, hh=hh, sb_=sb_, btm=btm: e.matmul(
                            sb_[0:64, hh * 64:(hh + 1) * 64],
                            btm, XDD[:, h * PH:(h + 1) * PH],
                            start=True, stop=True),
                            reads=(rbtm, rXDD), writes=(rsb,), tag="states")
                    P.op("PE", lambda e, q=q, cb_=cb_, cdr=cdr: e.matmul(
                        cb_[0:1, 0:512], cdr,
                        SGRP[:, q * 512:(q + 1) * 512], start=True, stop=True),
                        reads=(rcdr, rCONST), writes=(rcb,), tag="cdrow")
                    P.op("DVE", lambda e, cb_=cb_, cdrow=cdrow: e.tensor_copy(
                        cdrow, cb_[0:1, 0:512]),
                        reads=(rcb,), writes=(rcdrow,), tag="cdrowev")
                    P.op("PE", lambda e, cb_=cb_, cdrow=cdrow: e.matmul(
                        cb_[0:64, 0:512], ONESB[0:1, 0:64], cdrow,
                        start=True, stop=True),
                        reads=(rcdrow, rCONST), writes=(rcb,), tag="cdrep")
                    P.op("DVE", lambda e, q=q, cb_=cb_: e.tensor_tensor(
                        YSC[0:64, 0:512], PREV[0:64, q * 512:(q + 1) * 512],
                        cb_[0:64, 0:512], ALU.mult),
                        reads=(rPREV, rcb),
                        writes=(rYS, rYSl[0], rYSl[1], rYSl[2], rYSl[3]),
                        tag="scan1")
                    P.op("DVE", lambda e, q=q, sb_=sb_: e.tensor_tensor(
                        PREV[0:64, q * 512:(q + 1) * 512], YSC[0:64, 0:512],
                        sb_[0:64, 0:512], ALU.add),
                        reads=(rYS, rYSl[0], rsb), writes=(rPREV,), tag="scan2")
                    P.op("ACT", lambda e, q=q: e.activation(
                        PREVB[0:64, q * 512:(q + 1) * 512],
                        PREV[0:64, q * 512:(q + 1) * 512], AF.Copy),
                        reads=(rPREV,), writes=(rPREVB,), tag="prevb")
                for wv in range(2):
                    for dp in range(8):
                        d = wv * 8 + dp
                        P.op("PE", lambda e, d=d, dp=dp: e.transpose(
                            PBB[1][:, dp * 128:(dp + 1) * 128],
                            YTM[:, d * 128:(d + 1) * 128], IDB[:, :]),
                            reads=(rYTM, rCONST), writes=(rPB[1],), tag="yT")
                    P.op("DVE", lambda e, wv=wv, c0=c0: e.tensor_copy(
                        ARN[:, :].rearrange("p (t c) -> p t c", c=XW)
                        [:, wv * 8:(wv + 1) * 8, 4 + c0:4 + c0 + 128],
                        PBB[1][:, :].rearrange("p (t c) -> p t c", c=128)),
                        reads=(rPB[1],),
                        writes=(rARN,) + tuple(rARNl[wv * 8:wv * 8 + 8]),
                        tag="yTev2")
            for d in range(16):
                P.op("DVE", lambda e, d=d: e.scalar_tensor_tensor(
                    ARN[:, d * XW + 4:d * XW + 4 + TQ],
                    XBC[:, d * TQ:(d + 1) * TQ], DCOL[li][:, d:d + 1],
                    ARN[:, d * XW + 4:d * XW + 4 + TQ], ALU.mult, ALU.add),
                    reads=(rXBC, rARNl[d], rCONST),
                    writes=(rARN, rARNl[d]), tag="yD")
                P.op("DVE", lambda e, d=d: e.tensor_tensor(
                    ARN[:, d * XW + 4:d * XW + 4 + TQ],
                    ARN[:, d * XW + 4:d * XW + 4 + TQ],
                    SZ[:, d * TQ:(d + 1) * TQ], ALU.mult),
                    reads=(rARNl[d], rSZ), writes=(rARN, rARNl[d]), tag="ygate")
            dbg_store(f"ygate{li}_{qi}", ARN[:, :], (rARN,))
            for d in range(16):
                P.op("DVE", lambda e, d=d: e.tensor_tensor(
                    SQT[:, 0:512], ARN[:, d * XW + 4:d * XW + 4 + TQ],
                    ARN[:, d * XW + 4:d * XW + 4 + TQ], ALU.mult),
                    reads=(rARNl[d],), writes=(rSQT,), tag="yrsq")
                P.op("PE", lambda e, d=d: e.matmul(
                    PSB[2][0:1, 0:512], ONESB[:, 0:1], SQT[:, 0:512],
                    start=(d == 0), stop=(d == 15)),
                    reads=(rSQT, rCONST), writes=(rPS[2],), tag="yrmm")
            P.op("DVE", lambda e: e.tensor_copy(STAT[0:1, 0:512], PSB[2][0:1, 0:512]),
                 reads=(rPS[2],), writes=(rSTAT,), tag="yrev")
            rownorm_istd(DI, "yr")
            rank1_bcast(STATB[0:1, 512:1024], 3, (rSTAT, rYS), "yrbc")
            for d in range(16):
                P.op("DVE", lambda e, d=d: e.scalar_tensor_tensor(
                    ARN[:, d * XW + 4:d * XW + 4 + TQ],
                    ARN[:, d * XW + 4:d * XW + 4 + TQ],
                    RMSW[li][:, d:d + 1], PSB[3][:, 0:512], ALU.mult, ALU.mult),
                    reads=(rARNl[d], rPS[3], rCONST),
                    writes=(rARN, rARNl[d]), tag="yrap")

            for ob in range(8):
                wb, rwb = (WA, rWA) if ob % 2 == 0 else (WB, rWB)
                wblk_load(wb, rwb, outw, ob * DI * 128, DI, 128, tag=f"outw{ob}")
                pb, rpb = (PSB[0], rPS[0]) if ob % 2 == 0 else (PSB[1], rPS[1])
                for k in range(16):
                    P.op("PE", lambda e, k=k, wb=wb, pb=pb: e.matmul(
                        pb[:, 0:512], wb[:, k * 128:(k + 1) * 128],
                        ARN[:, k * XW + 4:k * XW + 4 + 512],
                        start=(k == 0), stop=(k == 15)),
                        reads=(rwb, rARNl[k]), writes=(rpb,), tag="outmm")
                P.op("DVE", lambda e, ob=ob, t0=t0, pb=pb: e.tensor_tensor(
                    RESID[:, ob * T + t0:ob * T + t0 + 512],
                    RESID[:, ob * T + t0:ob * T + t0 + 512],
                    pb[:, 0:512], ALU.add),
                    reads=(rRQ[qi], rpb), writes=(rRQ[qi],), tag="outev")
            dbg_store(f"res{li}_{qi}", RESID[:, :], rRESall)

    P.dma("SP", "usp", lambda e: e.dma_start(
        uspill.ap().rearrange("(a p) t -> p a t", p=128),
        RESID[:, :].rearrange("p (a t) -> p a t", t=T)),
        reads=rRESall, writes=(rUSP,), tag="uspill")

    def ln1_dst(k, t0, tp):
        P.op("DVE", lambda e, k=k, t0=t0: e.tensor_tensor(
            RESID[:, k * T + t0:k * T + t0 + 512], ULD[:, 0:512],
            PSB[3][:, 0:512], ALU.mult),
            reads=(rULD, rPS[3], rUSP), writes=(rRQ[tp],), tag="ln1w")

    layernorm(ln1_dst, "ln1")
    dbg_store("ln1", RESID[:, :], rRESall)

    attn_layer(0)
    attn_layer(1)

    for k in range(N_DMT):
        for hp in range(4):
            P.dma("SP", "uld", lambda e, k=k, hp=hp: e.dma_start(
                ULD[:, :], bass.AP(uspill, k * 128 * T + hp * 512,
                                   [[T, 128], [1, 512]])),
                reads=(rUSP,), writes=(rULD,), tag="uld")
            P.op("DVE", lambda e, k=k, hp=hp: e.tensor_tensor(
                RESID[:, k * T + hp * 512:k * T + (hp + 1) * 512],
                RESID[:, k * T + hp * 512:k * T + (hp + 1) * 512],
                ULD[:, :], ALU.add),
                reads=(rRQ[hp], rULD), writes=(rRQ[hp],), tag="x1")
    dbg_store("x1", RESID[:, :], rRESall)

    rBAR = res("attn_done")
    P.op("DVE", lambda e: e.engine_nop(),
         reads=(rPREV, rPREVB, rCDR, rCDROW, rBTM, rGTM, rGM[0], rGM[1],
                rXDT, rXDD, rYTM, rTMC, rDST, rDA, rSZ, rXBC, rARN, rNRM,
                rDIAG, rBCRAW, rBCC, rDGBC),
         writes=(rBAR,), tag="attnbar")

    for nm in reversed(("SZ", "XBC", "ARN", "NRM", "DIAG", "BCRAW", "BCC",
                        "DGBC", "DSTACK", "TMC", "XDT", "XDD", "YTM",
                        "GM", "GTM", "BTM", "PREV", "PREVB", "CDR", "CDROW")):
        sbuf_free(nm)
    LN2B = sbuf("LN2B", [128, N_DMT * T], BF16, side="right")
    ABUF = sbuf("ABUF", [128, N_DHT * TF], BF16, side="right")
    WH = [sbuf(f"WH{i}", [128, 32 * 128], BF16, side="right") for i in range(2)]
    rLN2 = res("LN2B"); rAB = res("ABUF"); rWH = [res("WH0"), res("WH1")]
    rABl = [res(f"ABUF{i}") for i in range(8)]
    rGBl = [res(f"GB{i}") for i in range(8)]
    GBUF = RESID
    rGB = rRES

    def ln2_dst(k, t0, tp):
        P.op("DVE", lambda e, k=k, t0=t0: e.tensor_tensor(
            LN2B[:, k * T + t0:k * T + t0 + 512], ULD[:, 0:512],
            PSB[3][:, 0:512], ALU.mult),
            reads=(rULD, rPS[3], rUSP, rBAR), writes=(rLN2,), tag="ln2w")

    layernorm(ln2_dst, "ln2")
    dbg_store("ln2", LN2B[:, :], (rLN2,))

    P.dma("SP", "usp", lambda e: e.dma_start(
        uspill.ap().rearrange("(a p) t -> p a t", p=128),
        RESID[:, :].rearrange("p (a t) -> p a t", t=T)),
        reads=rRESall, writes=(rUSP,), tag="x1spill")

    def gsl(dt_, c0, n=512):
        return GBUF[:, :].bitcast(BF16)[:, dt_ * TF + c0:dt_ * TF + c0 + n]

    for ch in range(2):
        ct0 = ch * TF
        for stage in range(2):
            wnm, snm = (("w1", "s1"), ("w2", "s2"))[stage]
            for obp in range(N_DHT * 2):
                ob = obp
                if ob >= N_DHT:
                    break
                if ob % 2 == 0:
                    wb, rwb = (WA, rWA) if (ob // 2) % 2 == 0 else (WB, rWB)
                    src_ap = bass.AP(dr[wnm], ob * DM * 128,
                                     [[128, 128], [DM * 128, 2], [128 * 128, 8],
                                      [1, 128]])
                    dst_ap = wb[:, 0:2048].rearrange("p (b k o) -> p b k o",
                                                     b=2, o=128)
                    P.dma("POOL", "wb_" + rwb.name,
                          lambda e, d=dst_ap, s=src_ap: e.dma_start(d, s),
                          reads=(rUSP, rBAR), writes=(rwb,), tag=wnm)
                wb, rwb = (WA, rWA) if (ob // 2) % 2 == 0 else (WB, rWB)
                boff = (ob % 2) * 1024
                for nh in range(2):
                    pb, rpb = (PSB[0], rPS[0]) if nh == 0 else (PSB[1], rPS[1])
                    c0 = nh * 512
                    for k in range(N_DMT):
                        P.op("PE", lambda e, k=k, c0=c0, ct0=ct0, wb=wb, pb=pb,
                             boff=boff:
                             e.matmul(pb[:, 0:512],
                                      wb[:, boff + k * 128:boff + (k + 1) * 128],
                                      LN2B[:, k * T + ct0 + c0:k * T + ct0 + c0 + 512],
                                      start=(k == 0), stop=(k == N_DMT - 1)),
                             reads=(rwb, rLN2), writes=(rpb,), tag=wnm + "mm")
                    if stage == 0:
                        P.op("ACT", lambda e, ob=ob, c0=c0, pb=pb: e.activation(
                            ABUF[:, ob * TF + c0:ob * TF + c0 + 512], pb[:, 0:512],
                            AF.Silu, scale=FSC["s1"][:, ob:ob + 1]),
                            reads=(rpb, rCONST, rUSP, rBAR),
                            writes=(rAB, rABl[ob // 4]), tag="aev")
                    else:
                        P.op("DVE", lambda e, ob=ob, c0=c0, pb=pb:
                             e.scalar_tensor_tensor(
                                 ABUF[:, ob * TF + c0:ob * TF + c0 + 512],
                                 pb[:, 0:512], FSC["s2"][:, ob:ob + 1],
                                 ABUF[:, ob * TF + c0:ob * TF + c0 + 512],
                                 ALU.mult, ALU.mult),
                             reads=(rpb, rCONST, rABl[ob // 4]),
                             writes=(rAB, rABl[ob // 4]), tag="hev")
        for stage in range(2):
            wnm, snm = (("h1", "sh1"), ("h2", "sh2"))[stage]
            for ob in range(N_DHT):
                wb, rwb = WH[ob % 2], rWH[ob % 2]
                wblk_load(wb, rwb, dr[wnm], ob * DH * 128, DH, 128,
                          extra_reads=(rUSP, rBAR), tag=wnm)
                for nh in range(2):
                    pb, rpb = (PSB[0], rPS[0]) if nh == 0 else (PSB[1], rPS[1])
                    c0 = nh * 512
                    for k in range(N_DHT):
                        if stage == 0:
                            rhs = ABUF[:, k * TF + c0:k * TF + c0 + 512]
                            rr = rABl[k // 4]
                        else:
                            rhs = gsl(k, c0)
                            rr = rGBl[k // 4]
                        P.op("PE", lambda e, k=k, wb=wb, pb=pb, rhs=rhs: e.matmul(
                            pb[:, 0:512], wb[:, k * 128:(k + 1) * 128], rhs,
                            start=(k == 0), stop=(k == N_DHT - 1)),
                            reads=(rwb, rr), writes=(rpb,), tag=wnm + "mm")
                    if stage == 0:
                        P.op("ACT", lambda e, ob=ob, c0=c0, pb=pb: e.activation(
                            gsl(ob, c0), pb[:, 0:512],
                            AF.Gelu, scale=FSC["sh1"][:, ob:ob + 1]),
                            reads=(rpb, rCONST),
                            writes=rRESall + (rGBl[ob // 4],), tag="gev")
                    else:
                        P.op("ACT", lambda e, ob=ob, c0=c0, pb=pb: e.activation(
                            ABUF[:, ob * TF + c0:ob * TF + c0 + 512], pb[:, 0:512],
                            AF.Gelu, scale=FSC["sh2"][:, ob:ob + 1]),
                            reads=(rpb, rCONST),
                            writes=(rAB, rABl[ob // 4]), tag="g2ev")
        for ob in range(N_DMT):
            wb, rwb = WH[ob % 2], rWH[ob % 2]
            wblk_load(wb, rwb, dr["w3"], ob * DH * 128, DH, 128,
                      extra_reads=(rUSP, rBAR), tag="w3")

            for nh in range(2):
                pb, rpb = (PSB[0], rPS[0]) if nh == 0 else (PSB[1], rPS[1])
                ev, rev = (YSC, rYS) if nh == 0 else (None, rMS)
                c0 = nh * 512
                P.dma("SP", "uld", lambda e, ob=ob, c0=c0, ct0=ct0:
                      e.dma_start(ULD[:, 0:512],
                                  bass.AP(uspill, ob * 128 * T + ct0 + c0,
                                          [[T, 128], [1, 512]])),
                      reads=(rUSP,), writes=(rULD,), tag="x1ld")
                for k in range(N_DHT):
                    P.op("PE", lambda e, k=k, c0=c0, wb=wb, pb=pb: e.matmul(
                        pb[:, 0:512], wb[:, k * 128:(k + 1) * 128],
                        ABUF[:, k * TF + c0:k * TF + c0 + 512],
                        start=(k == 0), stop=(k == N_DHT - 1)),
                        reads=(rwb, rABl[k // 4]), writes=(rpb,), tag="w3mm")
                evap = (YSC[:, 0:512] if nh == 0
                        else ESCR[:, :].bitcast(F32)[:, 0:512])
                P.op("DVE", lambda e, ob=ob, c0=c0, pb=pb, evap=evap:
                     e.scalar_tensor_tensor(
                         evap, pb[:, 0:512], FSC["s3"][:, ob:ob + 1],
                         ULD[:, 0:512],
                         ALU.mult, ALU.add),
                     reads=(rpb, rULD, rCONST),
                     writes=((rYS, rYSl[0], rYSl[1], rYSl[2], rYSl[3])
                             if nh == 0 else (rMS, rMS2, rES)), tag="w3ev")
                P.dma("SP", "out", lambda e, ob=ob, c0=c0, ct0=ct0, evap=evap:
                      e.dma_start(bass.AP(y_out, ob * 128 * T + ct0 + c0,
                                          [[T, 128], [1, 512]]), evap),
                      reads=((rYS,) if nh == 0 else (rMS,)), writes=(rOUT,),
                      tag="ystore")

    P.finalize()
    return nc, P



def emit(nc, P):
    from contextlib import ExitStack
    st = ExitStack()
    sems = {}
    for e in Prog.ENGS:
        sems[("E", e)] = st.enter_context(nc.semaphore("se_" + e))
    for s in P.dma_sems:
        sems[("D", s)] = st.enter_context(nc.semaphore("sd_" + s))
    block = st.enter_context(nc.Block())

    by_eng = {e: [] for e in Prog.ENGS}
    for o, w in zip(P.ops, P.waits):
        by_eng[o.eng].append((o, w))

    def run_stream(eng_obj, eng_name):
        for o, w in by_eng[eng_name]:
            for key, val in w.items():
                eng_obj.wait_ge(sems[key], val)
            inst = o.fn(eng_obj)
            if o.is_dma:
                inst.then_inc(sems[("D", o.sem)], 16)
            elif o.need_inc:
                inst.then_inc(sems[("E", o.eng)], 1)
        if eng_name == "SP":
            eng_obj.wait_ge(sems[("D", "out")], P.dma_sems["out"])
            if "dbgout" in P.dma_sems:
                eng_obj.wait_ge(sems[("D", "dbgout")], P.dma_sems["dbgout"])

    @block.tensor
    def _(e):
        run_stream(e, "PE")

    @block.vector
    def _(e):
        run_stream(e, "DVE")

    @block.scalar
    def _(e):
        run_stream(e, "ACT")

    @block.gpsimd
    def _(e):
        run_stream(e, "POOL")

    @block.sync
    def _(e):
        run_stream(e, "SP")

    st.close()


_CACHE = {}


def _get_program(debug_taps=()):
    key = tuple(debug_taps)
    if key not in _CACHE:
        nc, P = build_program(debug_taps)
        emit(nc, P)
        _CACHE[key] = nc
    return _CACHE[key]


def kernel(**inputs):
    per_core = _prep_inputs(inputs)
    nc = _get_program()
    res = run_bass_kernel_spmd(nc, per_core, core_ids=list(range(NCORES)))
    out = np.stack([np.ascontiguousarray(np.asarray(r["y"]).T)
                    for r in res.results])
    return out.astype(np.float32)


# version 21
# speedup vs baseline: 1.2967x; 1.0568x over previous
import numpy as np
import ml_dtypes
import concourse.bass as bass
import concourse.mybir as mybir
from concourse.bass_utils import run_bass_kernel_spmd

F32 = mybir.dt.float32
BF16 = mybir.dt.bfloat16
AF = mybir.ActivationFunctionType
ALU = mybir.AluOpType

DM = 1024
DI = 2048
H = 32
PH = 64
NS = 64
DC = 4
CS = 128
T = 2048
TQ = 512
NQ = T // TQ
CHQ = TQ // CS
DIN = 2 * DI + 2 * NS + H
DH = 4096
EPS = 1e-5
NCORES = 8
N_DMT = DM // 128
N_DHT = DH // 128
XW = 4 + TQ
TF = 1024

INP_BLOCKS = [128] * 16 + [128] * 16 + [64, 64, 32]



class Res:
    __slots__ = ("name", "writer", "readers")

    def __init__(self, name):
        self.name = name
        self.writer = None
        self.readers = []


class Op:
    __slots__ = ("idx", "eng", "fn", "deps", "is_dma", "sem", "semval",
                 "need_inc", "know", "tag")

    def __init__(self, idx, eng, fn, is_dma, tag):
        self.idx = idx
        self.eng = eng
        self.fn = fn
        self.deps = []
        self.is_dma = is_dma
        self.sem = None
        self.semval = None
        self.need_inc = False
        self.know = None
        self.tag = tag


class Prog:
    ENGS = ("PE", "DVE", "ACT", "POOL", "SP")

    def __init__(self):
        self.ops = []
        self.dma_sems = {}

    def op(self, eng, fn, reads=(), writes=(), tag=""):
        o = Op(len(self.ops), eng, fn, False, tag)
        self._wire(o, reads, writes)
        self.ops.append(o)
        return o

    def dma(self, queue, sem_name, fn, reads=(), writes=(), tag=""):
        o = Op(len(self.ops), queue, fn, True, tag)
        o.sem = sem_name
        self.dma_sems[sem_name] = self.dma_sems.get(sem_name, 0) + 16
        o.semval = self.dma_sems[sem_name]
        self._wire(o, reads, writes)
        self.ops.append(o)
        return o

    def _wire(self, o, reads, writes):
        dep = set()
        for r in reads:
            if r.writer is not None:
                dep.add(r.writer)
        for w in writes:
            if w.writer is not None:
                dep.add(w.writer)
            for rd in w.readers:
                dep.add(rd)
        dep.discard(o)
        o.deps = sorted(dep, key=lambda x: x.idx)
        for r in reads:
            r.readers.append(o)
        for w in writes:
            w.writer = o
            w.readers = []

    def finalize(self):
        def needs_edge(d, o):
            if d.is_dma:
                return True
            if d.eng == o.eng and d.eng == "PE":
                return False
            return True

        for o in self.ops:
            for d in o.deps:
                if (not d.is_dma) and needs_edge(d, o):
                    d.need_inc = True
        count = {e: 0 for e in self.ENGS}
        for o in self.ops:
            if (not o.is_dma) and o.need_inc:
                count[o.eng] += 1
                o.semval = count[o.eng]
        self.final_counts = dict(count)
        know = {e: {} for e in self.ENGS}
        waits = []
        for o in self.ops:
            k = know[o.eng]
            w = {}
            for d in o.deps:
                if not needs_edge(d, o):
                    continue
                if d.is_dma:
                    key, val = ("D", d.sem), d.semval
                else:
                    key, val = ("E", d.eng), d.semval
                    if val is None:
                        continue
                if k.get(key, 0) >= val:
                    continue
                if w.get(key, 0) < val:
                    w[key] = val
                for kk, vv in d.know.items():
                    if k.get(kk, 0) < vv:
                        k[kk] = vv
                k[key] = max(k.get(key, 0), val)
            waits.append(w)
            ksnap = dict(k)
            if o.is_dma:
                ksnap[("D", o.sem)] = o.semval
            elif o.need_inc:
                ksnap[("E", o.eng)] = o.semval
            o.know = ksnap
        self.waits = waits



def _blocked_wT(w, widths=None):
    wt = np.ascontiguousarray(np.asarray(w).T)
    if widths is None:
        widths = [128] * (np.asarray(w).shape[0] // 128)
    parts, o = [], 0
    for ow in widths:
        parts.append(np.ascontiguousarray(wt[:, o:o + ow]).reshape(-1))
        o += ow
    return np.concatenate(parts)


def _prep_inputs(inputs):
    g = {}
    for i in range(2):
        g[f"inw{i}"] = _blocked_wT(inputs["L0_attn_in_w"][i], INP_BLOCKS)
        g[f"outw{i}"] = _blocked_wT(inputs["L0_attn_out_w"][i])
        cw = np.asarray(inputs["L0_attn_conv_w"][i])
        g[f"convwx{i}"] = np.ascontiguousarray(
            cw[:DI].reshape(16, 128, DC).transpose(1, 0, 2).reshape(128, 16 * DC))
        g[f"convwbc{i}"] = np.ascontiguousarray(
            np.concatenate([cw[DI:DI + NS], cw[DI + NS:]], axis=1))
        cb = np.asarray(inputs["L0_attn_conv_b"][i])
        g[f"convbx{i}"] = np.ascontiguousarray(cb[:DI].reshape(16, 128).T)
        g[f"convbbc{i}"] = np.ascontiguousarray(
            np.stack([cb[DI:DI + NS], cb[DI + NS:]], axis=1))
        g[f"dtb{i}"] = np.ascontiguousarray(inputs["L0_attn_dt_bias"][i]).reshape(H, 1)
        g[f"alog{i}"] = np.ascontiguousarray(inputs["L0_attn_A_log"][i]).reshape(H, 1)
        g[f"drep{i}"] = np.ascontiguousarray(
            np.repeat(np.asarray(inputs["L0_attn_Dp"][i]), PH).reshape(16, 128).T)
        g[f"rmsw{i}"] = np.ascontiguousarray(
            np.asarray(inputs["L0_attn_rms_w"][i]).reshape(16, 128).T)
        g[f"normw{i}"] = np.ascontiguousarray(
            np.asarray(inputs["L0_attn_norm_w"][i]).reshape(8, 128).T)
    for nm, key in (("w1", "L0_ffn_w1_w"), ("w2", "L0_ffn_w2_w"),
                    ("h1", "L0_ffn_h1_w"), ("h2", "L0_ffn_h2_w"),
                    ("w3", "L0_ffn_w3_w")):
        g[nm] = _blocked_wT(inputs[key])
    for nm, key in (("s1", "L0_ffn_w1_s"), ("s2", "L0_ffn_w2_s"),
                    ("sh1", "L0_ffn_h1_s"), ("sh2", "L0_ffn_h2_s"),
                    ("s3", "L0_ffn_w3_s")):
        g[nm] = np.ascontiguousarray(np.asarray(inputs[key]).reshape(-1, 128).T)
    g["epsc"] = np.full((1, 1), EPS, dtype=np.float32)
    g["identb"] = np.eye(128, dtype=ml_dtypes.bfloat16)
    g["identf"] = np.eye(128, dtype=np.float32)
    g["onesb"] = np.ones((128, 128), dtype=ml_dtypes.bfloat16)
    m01 = np.zeros((128, 128), dtype=ml_dtypes.bfloat16)
    for j in range(128):
        m01[j, j:] = 1
    g["mask01"] = m01
    sg = np.zeros((H, 4, 512), dtype=ml_dtypes.bfloat16)
    for q in range(4):
        for hh in range(8):
            sg[q * 8 + hh, q, hh * 64:(hh + 1) * 64] = 1
    g["sgrp"] = np.ascontiguousarray(sg.reshape(H, 4 * 512))

    u = np.asarray(inputs["L0_attn_u"])
    per_core = []
    for c in range(NCORES):
        d = dict(g)
        d["u"] = np.ascontiguousarray(u[c].T)
        per_core.append(d)
    return per_core



def build_program(debug_taps=()):
    nc = bass.Bass("TRN2", target_bir_lowering=False)
    P = Prog()

    dr = {}
    def din(name, shape, dt=F32):
        dr[name] = nc.dram_tensor(name, shape, dt, kind="ExternalInput")

    din("u", [DM, T])
    for i in range(2):
        din(f"inw{i}", [DM * DIN]); din(f"outw{i}", [DI * DM])
        din(f"convwx{i}", [128, 16 * DC]); din(f"convwbc{i}", [NS, 2 * DC])
        din(f"convbx{i}", [128, 16]); din(f"convbbc{i}", [NS, 2])
        din(f"dtb{i}", [H, 1]); din(f"alog{i}", [H, 1])
        din(f"drep{i}", [128, 16]); din(f"rmsw{i}", [128, 16]); din(f"normw{i}", [128, 8])
    din("w1", [DM * DH]); din("w2", [DM * DH])
    din("h1", [DH * DH]); din("h2", [DH * DH]); din("w3", [DH * DM])
    for nm in ("s1", "s2", "sh1", "sh2"):
        din(nm, [128, DH // 128])
    din("s3", [128, DM // 128])
    din("epsc", [1, 1])
    din("identb", [128, 128], BF16); din("identf", [128, 128])
    din("onesb", [128, 128], BF16); din("mask01", [128, 128], BF16)
    din("sgrp", [H, 4 * 512], BF16)

    y_out = nc.dram_tensor("y", [DM, T], F32, kind="ExternalOutput")
    uspill = nc.dram_tensor("uspill", [DM, T], F32)
    dbg = {}
    for tname, shape in debug_taps:
        dbg[tname] = nc.dram_tensor("dbg_" + tname, list(shape), F32,
                                    kind="ExternalOutput")

    live_cms = {}
    def sbuf(name, shape, dt, side="left"):
        cm = nc.sbuf_tensor(name, shape, dt, side=side)
        h = cm.__enter__()
        live_cms[name] = cm
        return h

    def sbuf_free(name):
        live_cms.pop(name).__exit__(None, None, None)

    RESID = sbuf("RESID", [128, N_DMT * T], F32)
    STAT = sbuf("STAT", [1, 1024], F32)
    ULD = sbuf("ULD", [128, 512], F32)
    YSC = sbuf("YSC", [128, 512], F32)
    ESCR = sbuf("ESCR", [128, 1024], BF16)
    SQT = sbuf("SQT", [128, 512], BF16)
    WA = sbuf("WA", [128, 16 * 128], BF16)
    WB = sbuf("WB", [128, 16 * 128], BF16)
    WC = sbuf("WC", [128, 16 * 128], BF16)
    EPSC = sbuf("EPSC", [1, 1], F32)
    IDB = sbuf("IDB", [128, 128], BF16)
    IDF = sbuf("IDF", [128, 128], F32)
    ONESB = sbuf("ONESB", [128, 128], BF16)
    MASK01 = sbuf("MASK01", [128, 128], BF16)
    SGRP = sbuf("SGRP", [H, 4 * 512], BF16)
    ONES32F = sbuf("ONES32F", [32, CS], F32)
    FSC = {nm: sbuf("FSC_" + nm, [128, (DH if nm != "s3" else DM) // 128], F32)
           for nm in ("s1", "s2", "sh1", "sh2", "s3")}
    CWX = [sbuf(f"CWX{i}", [128, 16 * DC], F32) for i in range(2)]
    CWBC = [sbuf(f"CWBC{i}", [64, 2 * DC], F32) for i in range(2)]
    CBX = [sbuf(f"CBX{i}", [128, 16], F32) for i in range(2)]
    CBBC = [sbuf(f"CBBC{i}", [64, 2], F32) for i in range(2)]
    DTB = [sbuf(f"DTB{i}", [H, 1], F32) for i in range(2)]
    ALOG = [sbuf(f"ALOG{i}", [H, 1], F32) for i in range(2)]
    AEXP = [sbuf(f"AEXP{i}", [H, 1], F32) for i in range(2)]
    DCOL = [sbuf(f"DCOL{i}", [128, 16], F32) for i in range(2)]
    RMSW = [sbuf(f"RMSW{i}", [128, 16], F32) for i in range(2)]
    NORMW = [sbuf(f"NORMW{i}", [128, 8], F32) for i in range(2)]

    SZ = sbuf("SZ", [128, 16 * TQ], BF16, side="right")
    XBC = sbuf("XBC", [128, 16 * TQ], BF16, side="right")
    ARN = sbuf("ARN", [128, 16 * XW], BF16, side="right")
    NRM = sbuf("NRM", [128, N_DMT * TQ], BF16, side="right")
    DIAG = sbuf("DIAG", [128, 64 * 128], BF16, side="right")
    BCRAW = sbuf("BCRAW", [64, 2 * XW], BF16, side="right")
    BCC = sbuf("BCC", [64, 2 * TQ], BF16, side="right")
    DGBC = sbuf("DGBC", [64, 8 * 64], BF16, side="right")
    DSTACK = sbuf("DSTACK", [128, TQ], F32, side="right")
    TMC = sbuf("TMC", [128, 4 * 192], F32, side="right")
    XDT = sbuf("XDT", [128, DI], BF16, side="right")
    XDD = sbuf("XDD", [128, DI], BF16, side="right")
    YTM = sbuf("YTM", [128, DI], BF16, side="right")
    GM = sbuf("GM", [128, 3 * 512], BF16, side="right")
    GTM = sbuf("GTM", [128, 4 * 128], BF16, side="right")
    BTM = sbuf("BTM", [128, 4 * NS], BF16, side="right")
    PREV = sbuf("PREV", [64, H * PH], F32, side="right")
    PREVB = sbuf("PREVB", [64, H * PH], BF16, side="right")
    CDR = sbuf("CDR", [32, 4], BF16, side="right")
    CDROW = sbuf("CDROW", [1, 2 * 512], BF16, side="right")

    PSB = [nc.psum_tensor(f"PSB{i}", [128, 512], F32).__enter__() for i in range(6)]
    PBB = [nc.psum_tensor(f"PBB{i}", [128, 1024], BF16).__enter__() for i in range(2)]

    R = {}
    def res(name):
        if name not in R:
            R[name] = Res(name)
        return R[name]

    rRES = res("RESID")
    rRQ = [res(f"RESQ{i}") for i in range(4)]
    rRESall = (rRES, rRQ[0], rRQ[1], rRQ[2], rRQ[3])
    rSZ = res("SZ"); rXBC = res("XBC"); rARN = res("ARN")
    rNRM = res("NRM"); rDIAG = res("DIAG"); rBCRAW = res("BCRAW"); rBCC = res("BCC")
    rDGBC = res("DGBC"); rDST = res("DSTACK"); rDA = res("DAROW"); rTMC = res("TMC")
    rXDT = res("XDT"); rXDD = res("XDD"); rYTM = res("YTM")
    rGM = [res("GM0"), res("GM1"), res("GM2")]
    rARNl = [res(f"ARN{i}") for i in range(16)]
    rNRMl = [res(f"NRM{i}") for i in range(8)]
    rARNall = (rARN,) + tuple(rARNl)
    rTMCl = [res(f"TMC{i}") for i in range(4)]
    rGTMl = [res(f"GTM{i}") for i in range(4)]
    rBTMl = [res(f"BTM{i}") for i in range(4)]
    rCDRl = [res(f"CDR{i}") for i in range(4)]
    rCDROW2 = res("CDROW2"); rMS2 = res("MSCR2"); rMS3 = res("MSCR3")
    rGTM = res("GTM"); rBTM = res("BTM"); rPREV = res("PREV"); rPREVB = res("PREVB")
    rMS = res("MSCR"); rES = res("ESCR"); rYS = res("YSC"); rCDR = res("CDR")
    rYSl = [res(f"YSC{i}") for i in range(4)]
    rCDROW = res("CDROW"); rSTAT = res("STAT"); rSQT = res("SQT"); rULD = res("ULD")
    rWA = res("WA"); rWB = res("WB"); rWC = res("WC")
    rPS = [res(f"PSB{i}") for i in range(6)]
    rPB = [res(f"PBB{i}") for i in range(2)]
    rCONST = res("CONST")
    rUSP = res("USPILL")
    rOUT = res("OUT")

    def setup_load(dst_ap, src_ap):
        P.dma("SP", "set", lambda e, d=dst_ap, s=src_ap: e.dma_start(d, s),
              reads=(), writes=(rCONST,), tag="setup")

    setup_load(EPSC[:, :], dr["epsc"].ap())
    setup_load(IDB[:, :], dr["identb"].ap())
    setup_load(IDF[:, :], dr["identf"].ap())
    setup_load(ONESB[:, :], dr["onesb"].ap())
    setup_load(MASK01[:, :], dr["mask01"].ap())
    setup_load(SGRP[:, :], dr["sgrp"].ap())
    for i in range(2):
        setup_load(CWX[i][:, :], dr[f"convwx{i}"].ap())
        setup_load(CWBC[i][:, :], dr[f"convwbc{i}"].ap())
        setup_load(CBX[i][:, :], dr[f"convbx{i}"].ap())
        setup_load(CBBC[i][:, :], dr[f"convbbc{i}"].ap())
        setup_load(DTB[i][:, :], dr[f"dtb{i}"].ap())
        setup_load(ALOG[i][:, :], dr[f"alog{i}"].ap())
        setup_load(DCOL[i][:, :], dr[f"drep{i}"].ap())
        setup_load(RMSW[i][:, :], dr[f"rmsw{i}"].ap())
        setup_load(NORMW[i][:, :], dr[f"normw{i}"].ap())
    for nm in ("s1", "s2", "sh1", "sh2", "s3"):
        setup_load(FSC[nm][:, :], dr[nm].ap())
    P.dma("SP", "set", lambda e: e.dma_start(
        RESID[:, :].rearrange("p (a t) -> p a t", t=T),
        dr["u"].ap().rearrange("(a p) t -> p a t", p=128)),
        reads=(), writes=rRESall + (rCONST,), tag="uload")

    P.op("DVE", lambda e: e.memset(ONES32F[:, :], 1.0),
         reads=(rCONST,), writes=(rCONST,), tag="ones32")
    for i in range(2):
        P.op("ACT", lambda e, i=i: e.activation(AEXP[i][:, :], ALOG[i][:, :], AF.Exp),
             reads=(rCONST,), writes=(rCONST,), tag="aexp")

    def dbg_store(name, sb_ap, reads):
        if name in dbg:
            P.dma("POOL", "dbgout", lambda e, d=dbg[name], s=sb_ap: e.dma_start(d.ap(), s),
                  reads=tuple(reads), writes=(rOUT,), tag="dbg")

    def wblk_load(wb, rwb, dram, elem_off, K, ow, extra_reads=(), tag="wl"):
        kt = K // 128
        src = bass.AP(dram, elem_off, [[ow, 128], [128 * ow, kt], [1, ow]])
        dst = wb[:, 0:kt * ow].rearrange("p (k o) -> p k o", o=ow)
        P.dma("POOL", "wb_" + rwb.name,
              lambda e, d=dst, s=src: e.dma_start(d, s),
              reads=tuple(extra_reads), writes=(rwb,), tag=tag)

    def rank1_bcast(row_ap, bank, reads, tag):
        P.op("PE", lambda e, r=row_ap: e.matmul(
            PSB[bank][:, 0:512], ONESB[0:1, :], r, start=True, stop=True),
            reads=tuple(reads) + (rCONST,), writes=(rPS[bank],), tag=tag)

    STATB = YSC[0:1, 0:512].bitcast(BF16)

    def rownorm_istd(nelem, tag):
        P.op("ACT", lambda e: e.activation(
            STAT[0:1, 0:512], STAT[0:1, 0:512], AF.Ln, bias=EPSC[0:1, 0:1],
            scale=1.0 / nelem),
            reads=(rSTAT, rCONST), writes=(rSTAT,), tag=tag + "ln")
        P.op("ACT", lambda e: e.activation(
            STATB[0:1, 512:1024], STAT[0:1, 0:512], AF.Exp, scale=-0.5),
            reads=(rSTAT,),
            writes=(rSTAT, rYS, rYSl[0], rYSl[1], rYSl[2], rYSl[3]),
            tag=tag + "exp")

    def layernorm(dst_fn, tag):
        for tp in range(T // 512):
            t0 = tp * 512
            for k in range(N_DMT):
                P.op("ACT", lambda e, k=k, t0=t0: e.activation(
                    SQT[:, 0:512], RESID[:, k * T + t0:k * T + t0 + 512],
                    AF.Square),
                    reads=(rRQ[tp],), writes=(rSQT,), tag=tag + "sq")
                P.op("DVE", lambda e, k=k, t0=t0: e.tensor_copy(
                    ESCR[:, 0:512], RESID[:, k * T + t0:k * T + t0 + 512]),
                    reads=(rRQ[tp],), writes=(rES, rMS, rMS2), tag=tag + "cp")
                P.op("PE", lambda e, k=k: e.matmul(
                    PSB[2][0:1, 0:512], ONESB[:, 0:1], SQT[:, 0:512],
                    start=(k == 0), stop=(k == N_DMT - 1)),
                    reads=(rSQT, rCONST), writes=(rPS[2],), tag=tag + "mmsq")
                P.op("PE", lambda e, k=k: e.matmul(
                    PSB[5][0:1, 0:512], ONESB[:, 0:1], ESCR[:, 0:512],
                    start=(k == 0), stop=(k == N_DMT - 1)),
                    reads=(rES, rCONST), writes=(rPS[5],), tag=tag + "mmsum")
            P.op("DVE", lambda e: e.tensor_copy(STAT[0:1, 0:512], PSB[2][0:1, 0:512]),
                 reads=(rPS[2],), writes=(rSTAT,), tag=tag + "evA")
            P.op("DVE", lambda e: e.tensor_scalar(
                STAT[0:1, 512:1024], PSB[5][0:1, 0:512], 1.0 / DM, None, ALU.mult),
                reads=(rPS[5],), writes=(rSTAT,), tag=tag + "mean")
            P.op("DVE", lambda e: e.tensor_tensor(
                YSC[0:1, 0:512], STAT[0:1, 512:1024], STAT[0:1, 512:1024],
                ALU.mult),
                reads=(rSTAT,), writes=(rYS,), tag=tag + "m2")
            P.op("DVE", lambda e: e.scalar_tensor_tensor(
                STAT[0:1, 0:512], STAT[0:1, 0:512], 1.0 / DM,
                YSC[0:1, 0:512], ALU.mult, ALU.subtract),
                reads=(rSTAT, rYS), writes=(rSTAT,), tag=tag + "var")
            P.op("DVE", lambda e: e.tensor_copy(
                STATB[0:1, 0:512], STAT[0:1, 512:1024]),
                reads=(rSTAT,),
                writes=(rSTAT, rYS, rYSl[0], rYSl[1], rYSl[2], rYSl[3]),
                tag=tag + "meanb")
            rownorm_istd(1, tag)
            rank1_bcast(STATB[0:1, 512:1024], 3, (rSTAT, rYS), tag + "bi")
            rank1_bcast(STATB[0:1, 0:512], 4, (rSTAT, rYS), tag + "bm")
            for k in range(N_DMT):
                P.op("DVE", lambda e, k=k, t0=t0: e.tensor_tensor(
                    ULD[:, 0:512], RESID[:, k * T + t0:k * T + t0 + 512],
                    PSB[4][:, 0:512], ALU.subtract),
                    reads=(rRQ[tp], rPS[4]), writes=(rULD,), tag=tag + "sub")
                dst_fn(k, t0, tp)

    def attn_layer(li):
        inw = dr[f"inw{li}"]; outw = dr[f"outw{li}"]
        P.op("DVE", lambda e: e.memset(PREV[0:64, :], 0.0),
             reads=(), writes=(rPREV,), tag="prev0")
        P.op("DVE", lambda e: e.memset(PREVB[0:64, :], 0.0),
             reads=(), writes=(rPREVB,), tag="prevb0")
        for t in range(16):
            for w in range(DC):
                P.op("ACT", lambda e, t=t, w=w: e.activation(
                    DIAG[:, (t * DC + w) * 128:(t * DC + w + 1) * 128],
                    IDB[:, :], AF.Copy,
                    scale=CWX[li][:, t * DC + w:t * DC + w + 1]),
                    reads=(rCONST,), writes=(rDIAG,), tag="diagx")
        for bc in range(2):
            for w in range(DC):
                P.op("ACT", lambda e, bc=bc, w=w: e.activation(
                    DGBC[0:64, (bc * DC + w) * 64:(bc * DC + w + 1) * 64],
                    IDB[0:64, 0:64], AF.Copy,
                    scale=CWBC[li][0:64, bc * DC + w:bc * DC + w + 1]),
                    reads=(rCONST,), writes=(rDGBC,), tag="diagbc")

        blk_off = [0]
        for ow in INP_BLOCKS:
            blk_off.append(blk_off[-1] + DM * ow)

        for qi in range(NQ):
            t0 = qi * TQ
            for k in range(N_DMT):
                P.op("ACT", lambda e, k=k, t0=t0: e.activation(
                    SQT[:, 0:512], RESID[:, k * T + t0:k * T + t0 + 512],
                    AF.Square),
                    reads=(rRQ[qi],), writes=(rSQT,), tag="rmsq")
                P.op("PE", lambda e, k=k: e.matmul(
                    PSB[2][0:1, 0:512], ONESB[:, 0:1], SQT[:, 0:512],
                    start=(k == 0), stop=(k == N_DMT - 1)),
                    reads=(rSQT, rCONST), writes=(rPS[2],), tag="rmsmm")
            P.op("DVE", lambda e: e.tensor_copy(STAT[0:1, 0:512], PSB[2][0:1, 0:512]),
                 reads=(rPS[2],), writes=(rSTAT,), tag="rmsev")
            rownorm_istd(DM, "rms")
            rank1_bcast(STATB[0:1, 512:1024], 3, (rSTAT, rYS), "rmsbc")
            for k in range(N_DMT):
                P.op("DVE", lambda e, k=k, t0=t0: e.scalar_tensor_tensor(
                    NRM[:, k * TQ:(k + 1) * TQ],
                    RESID[:, k * T + t0:k * T + t0 + 512],
                    NORMW[li][:, k:k + 1], PSB[3][:, 0:512],
                    ALU.mult, ALU.mult),
                    reads=(rRQ[qi], rPS[3], rCONST),
                    writes=(rNRM, rNRMl[k]), tag="rmsap")
            dbg_store(f"normed{li}_{qi}", NRM[:, :], (rNRM,))

            if qi == 0:
                P.op("DVE", lambda e: e.memset(ARN[:, :], 0.0),
                     reads=(rARN,), writes=rARNall, tag="tail0")
                P.op("DVE", lambda e: e.memset(BCRAW[0:64, :], 0.0),
                     reads=(rBCRAW,), writes=(rBCRAW,), tag="tail0bc")

            groups = [(34,), (32, 33)] + [(2 * i, 2 * i + 1) for i in range(16)]
            wrot = [(WA, rWA), (WB, rWB), (WC, rWC)]

            def emit_inproj_group(gi, grp):
                wb, rwb = wrot[gi % 3]
                nb = len(grp)
                ow = INP_BLOCKS[grp[0]]
                src = bass.AP(inw, blk_off[grp[0]],
                              [[ow, 128], [DM * ow, nb], [128 * ow, DM // 128],
                               [1, ow]])
                dst = wb[:, 0:8 * nb * ow].rearrange("p (b k o) -> p b k o",
                                                     b=nb, o=ow)
                P.dma("POOL", "wb_" + rwb.name,
                      lambda e, d=dst, s=src: e.dma_start(d, s),
                      reads=(), writes=(rwb,), tag=f"inwg{gi}")
                for bi, ob in enumerate(grp):
                    pb, rpb = (PSB[0], rPS[0]) if ob % 2 == 0 else (PSB[1], rPS[1])
                    for k in range(N_DMT):
                        P.op("PE", lambda e, ow=ow, k=k, bi=bi, wb=wb, pb=pb:
                             e.matmul(pb[0:ow, 0:512],
                                      wb[:, (bi * 8 + k) * ow:(bi * 8 + k + 1) * ow],
                                      NRM[:, k * TQ:k * TQ + 512],
                                      start=(k == 0), stop=(k == N_DMT - 1)),
                             reads=(rwb, rNRMl[k]), writes=(rpb,), tag="inpmm")
                    if ob < 16:
                        P.op("ACT", lambda e, ob=ob, pb=pb: e.activation(
                            SZ[:, ob * TQ:(ob + 1) * TQ], pb[:, 0:512], AF.Silu),
                            reads=(rpb,), writes=(rSZ,), tag="zev")
                    elif ob < 32:
                        t = ob - 16
                        P.op("DVE", lambda e, t=t, pb=pb: e.tensor_copy(
                            ARN[:, t * XW + 4:t * XW + 4 + 512], pb[:, 0:512]),
                            reads=(rpb,), writes=(rARN, rARNl[t]), tag="xev")
                    elif ob < 34:
                        bc = ob - 32
                        P.op("DVE", lambda e, bc=bc, pb=pb: e.tensor_copy(
                            BCRAW[0:64, bc * XW + 4:bc * XW + 4 + 512],
                            pb[0:64, 0:512]),
                            reads=(rpb,), writes=(rBCRAW,), tag="bcev")
                    else:
                        P.op("DVE", lambda e, pb=pb: e.tensor_copy(
                            DSTACK[0:32, 0:512], pb[0:32, 0:512]),
                            reads=(rpb,), writes=(rDST,), tag="dtev")

            def emit_bcconv_dt():
                for bc in range(2):
                    pb, rpb = (PSB[4], rPS[4]) if bc == 0 else (PSB[5], rPS[5])
                    for w in range(DC):
                        P.op("PE", lambda e, bc=bc, w=w, pb=pb: e.matmul(
                            pb[0:64, 0:512],
                            DGBC[0:64, (bc * DC + w) * 64:(bc * DC + w + 1) * 64],
                            BCRAW[0:64, bc * XW + 1 + w:bc * XW + 1 + w + 512],
                            start=(w == 0), stop=(w == DC - 1)),
                            reads=(rDGBC, rBCRAW), writes=(rpb,), tag="convmmbc")
                    P.op("ACT", lambda e, bc=bc, pb=pb: e.activation(
                        BCC[0:64, bc * TQ:(bc + 1) * TQ], pb[0:64, 0:512],
                        AF.Silu, bias=CBBC[li][0:64, bc:bc + 1]),
                        reads=(rpb, rCONST), writes=(rBCC,), tag="convevbc")
                P.op("ACT", lambda e: e.activation(
                    YSC[0:32, :], DSTACK[0:32, :], AF.Abs, bias=DTB[li][:, :]),
                    reads=(rDST, rCONST),
                    writes=(rYS, rYSl[0], rYSl[1], rYSl[2], rYSl[3]),
                    tag="sp_abs")
                P.op("ACT", lambda e: e.activation(
                    YSC[0:32, :], YSC[0:32, :], AF.Exp, scale=-1.0),
                    reads=(rYS,), writes=(rYS,), tag="sp_exp")
                P.op("ACT", lambda e: e.activation(
                    YSC[0:32, :], YSC[0:32, :], AF.Ln, bias=1.0),
                    reads=(rYS,), writes=(rYS,), tag="sp_ln")
                P.op("ACT", lambda e: e.activation(
                    DSTACK[0:32, :], DSTACK[0:32, :], AF.Relu, bias=DTB[li][:, :]),
                    reads=(rDST, rCONST), writes=(rDST,), tag="sp_relu")
                P.op("DVE", lambda e: e.tensor_tensor(
                    DSTACK[0:32, :], DSTACK[0:32, :], YSC[0:32, :], ALU.add),
                    reads=(rDST, rYS), writes=(rDST,), tag="sp_add")
                P.op("DVE", lambda e: e.tensor_scalar(
                    YSC[0:32, :], DSTACK[0:32, :], AEXP[li][:, :], -1.0,
                    ALU.mult, ALU.mult),
                    reads=(rDST, rCONST), writes=(rYS,), tag="dA")

            def emit_front(c):
                c0, c1 = c * CS, (c + 1) * CS
                tmc = TMC[:, c * 192:c * 192 + 192]
                rtmc = rTMCl[c]
                gtm = GTM[:, c * 128:c * 128 + 128]
                rgtm = rGTMl[c]
                btm = BTM[:, c * NS:c * NS + NS]
                rbtm = rBTMl[c]
                cdr = CDR[0:32, c:c + 1]
                rcdr = rCDRl[c]
                P.op("DVE", lambda e, c0=c0, c1=c1: e.tensor_tensor_scan(
                    DSTACK[32:64, c0:c1], ONES32F[:, :], YSC[0:32, c0:c1],
                    0.0, ALU.mult, ALU.add),
                    reads=(rYS, rCONST), writes=(rDST,), tag="cumsum")
                P.op("ACT", lambda e, c0=c0, c1=c1: e.activation(
                    DSTACK[64:96, c0:c1], DSTACK[32:64, c0:c1], AF.Exp,
                    scale=-1.0, bias=DSTACK[32:64, c1 - 1:c1]),
                    reads=(rDST,), writes=(rDST,), tag="decay")
                P.op("ACT", lambda e, c0=c0, c1=c1: e.activation(
                    DSTACK[96:128, c0:c1], DSTACK[32:64, c0:c1], AF.Exp),
                    reads=(rDST,), writes=(rDST,), tag="ecum")
                P.op("PE", lambda e, c0=c0, c1=c1: e.transpose(
                    PSB[4][:, 0:128], DSTACK[:, c0:c1], IDF[:, :]),
                    reads=(rDST, rCONST), writes=(rPS[4],), tag="stackT")
                P.op("DVE", lambda e, tmc=tmc: e.tensor_copy(
                    tmc[:, 0:128], PSB[4][:, 0:128]),
                    reads=(rPS[4],), writes=(rtmc,), tag="stackEv")
                P.op("DVE", lambda e, tmc=tmc: e.tensor_tensor(
                    tmc[:, 128:160], tmc[:, 0:32], tmc[:, 64:96], ALU.mult),
                    reads=(rtmc,), writes=(rtmc,), tag="dtdecay")
                P.op("ACT", lambda e, tmc=tmc: e.activation(
                    tmc[:, 160:192], tmc[:, 0:32], AF.Ln),
                    reads=(rtmc,), writes=(rtmc,), tag="lndt")
                P.op("DVE", lambda e, tmc=tmc: e.tensor_tensor(
                    tmc[:, 160:192], tmc[:, 160:192], tmc[:, 32:64],
                    ALU.subtract),
                    reads=(rtmc,), writes=(rtmc,), tag="bias2")
                P.op("PE", lambda e, c0=c0, c1=c1: e.matmul(
                    PSB[5][:, 0:128], BCC[0:64, c0:c1],
                    BCC[0:64, TQ + c0:TQ + c1], start=True, stop=True),
                    reads=(rBCC,), writes=(rPS[5],), tag="gt")
                P.op("DVE", lambda e, gtm=gtm: e.tensor_tensor(
                    gtm, PSB[5][:, 0:128], MASK01[:, :], ALU.mult),
                    reads=(rPS[5], rCONST), writes=(rgtm,), tag="gtev")
                P.op("PE", lambda e, c0=c0, c1=c1: e.transpose(
                    PBB[1][:, 0:64], BCC[0:64, c0:c1], IDB[0:64, 0:64]),
                    reads=(rBCC, rCONST), writes=(rPB[1],), tag="btmT")
                P.op("DVE", lambda e, btm=btm: e.tensor_copy(btm, PBB[1][:, 0:64]),
                     reads=(rPB[1],), writes=(rbtm,), tag="btmEv")

            for gi, grp in enumerate(groups):
                emit_inproj_group(gi, grp)
                if gi == 1:
                    emit_bcconv_dt()
                if gi == 5:
                    for c in range(CHQ):
                        emit_front(c)

            for t in range(16):
                pb, rpb = (PSB[0], rPS[0]) if t % 2 == 0 else (PSB[1], rPS[1])
                for w in range(DC):
                    P.op("PE", lambda e, t=t, w=w, pb=pb: e.matmul(
                        pb[:, 0:512],
                        DIAG[:, (t * DC + w) * 128:(t * DC + w + 1) * 128],
                        ARN[:, t * XW + 1 + w:t * XW + 1 + w + 512],
                        start=(w == 0), stop=(w == DC - 1)),
                        reads=(rDIAG, rARNl[t]), writes=(rpb,), tag="convmm")
                P.op("ACT", lambda e, t=t, pb=pb: e.activation(
                    XBC[:, t * TQ:(t + 1) * TQ], pb[:, 0:512],
                    AF.Silu, bias=CBX[li][:, t:t + 1]),
                    reads=(rpb, rCONST), writes=(rXBC,), tag="convev")
            P.op("DVE", lambda e: e.tensor_copy(
                ARN[:, :].rearrange("p (t c) -> p t c", c=XW)[:, :, 1:4],
                ARN[:, :].rearrange("p (t c) -> p t c", c=XW)[:, :, XW - 3:XW]),
                reads=(rARN,), writes=rARNall, tag="tail")
            P.op("DVE", lambda e: e.tensor_copy(
                BCRAW[0:64, :].rearrange("p (t c) -> p t c", c=XW)[:, :, 1:4],
                BCRAW[0:64, :].rearrange("p (t c) -> p t c", c=XW)[:, :, XW - 3:XW]),
                reads=(rBCRAW,), writes=(rBCRAW,), tag="tailbc")
            dbg_store(f"convx{li}_{qi}", XBC[:, :], (rXBC,))

            for c in range(CHQ):
                c0, c1 = c * CS, (c + 1) * CS
                par = c % 2
                tmc = TMC[:, c * 192:c * 192 + 192]
                rtmc = rTMCl[c]
                gtm = GTM[:, c * 128:c * 128 + 128]
                rgtm = rGTMl[c]
                btm = BTM[:, c * NS:c * NS + NS]
                rbtm = rBTMl[c]
                cdr = CDR[0:32, c:c + 1]
                rcdr = rCDRl[c]
                cdrow = CDROW[0:1, par * 512:par * 512 + 512]
                rcdrow = rCDROW if par == 0 else rCDROW2
                for wv in range(2):
                    for dp in range(8):
                        d = wv * 8 + dp
                        P.op("PE", lambda e, d=d, dp=dp, c0=c0, c1=c1: e.transpose(
                            PBB[0][:, dp * 128:(dp + 1) * 128],
                            XBC[:, d * TQ + c0:d * TQ + c1], IDB[:, :]),
                            reads=(rXBC, rCONST), writes=(rPB[0],), tag="xT")
                    P.op("DVE", lambda e, wv=wv: e.tensor_copy(
                        XDT[:, wv * 1024:(wv + 1) * 1024], PBB[0][:, :]),
                        reads=(rPB[0],), writes=(rXDT,), tag="xev2")
                for sub in range(4):
                    P.op("DVE", lambda e, sub=sub, tmc=tmc: e.tensor_tensor(
                        XDD[:, sub * 512:(sub + 1) * 512].rearrange(
                            "p (h j) -> p h j", j=PH),
                        XDT[:, sub * 512:(sub + 1) * 512].rearrange(
                            "p (h j) -> p h j", j=PH),
                        tmc[:, 128 + sub * 8:136 + sub * 8].unsqueeze(2)
                        .broadcast_to([128, 8, PH]), ALU.mult),
                        reads=(rXDT, rtmc), writes=(rXDD,), tag="xdd")
                PBB1F = PBB[1][:, :].bitcast(F32)
                MB3 = [PSB[4], PSB[5], PBB1F]
                rMB3 = [rPS[4], rPS[5], rPB[1]]
                rMSl = [rMS, rMS2, rMS3]

                def mask_front(g):
                    mi = g % 3
                    mb, rmb = MB3[mi], rMB3[mi]
                    for h4 in range(4):
                        h = 4 * g + h4
                        P.op("PE", lambda e, h=h, h4=h4, mb=mb, tmc=tmc: e.transpose(
                            mb[:, h4 * 128:(h4 + 1) * 128],
                            tmc[:, 32 + h:32 + h + 1].broadcast_to([128, 128]),
                            IDF[:, :]),
                            reads=(rtmc, rCONST), writes=(rmb,), tag="mbc")

                def mask_mid(g):
                    mi = g % 3
                    mb, rmb = MB3[mi], rMB3[mi]
                    es = (ESCR[:, mi * 512:(mi + 1) * 512] if mi < 2
                          else SQT[:, 0:512])
                    res_ = rMSl[mi]
                    esw = (res_,) if mi < 2 else (res_, rSQT)
                    gmap = GM[:, mi * 512:(mi + 1) * 512]
                    rgm = rGM[mi]
                    P.op("DVE", lambda e, g=g, mb=mb, tmc=tmc: e.tensor_tensor(
                        mb[:, 0:512].rearrange("p (h j) -> p h j", j=128),
                        mb[:, 0:512].rearrange("p (h j) -> p h j", j=128),
                        tmc[:, 160 + 4 * g:164 + 4 * g].unsqueeze(2)
                        .broadcast_to([128, 4, 128]), ALU.add),
                        reads=(rmb, rtmc), writes=(rmb,), tag="md2a")
                    P.op("DVE", lambda e, mb=mb: e.tensor_scalar(
                        mb[:, 0:512], mb[:, 0:512], 30.0, None, ALU.min),
                        reads=(rmb,), writes=(rmb,), tag="md2b")
                    P.op("ACT", lambda e, mb=mb, es=es: e.activation(
                        es, mb[:, 0:512], AF.Exp),
                        reads=(rmb,), writes=esw, tag="mexp")
                    P.op("DVE", lambda e, es=es, gmap=gmap, gtm=gtm:
                         e.tensor_tensor(
                             gmap.rearrange("p (r n) -> p r n", r=4),
                             es.rearrange("p (r n) -> p r n", r=4),
                             gtm.unsqueeze(1).broadcast_to([128, 4, 128]),
                             ALU.mult),
                         reads=esw + (rgtm,), writes=(rgm,), tag="mgm")


                def yoff_grp(g):
                    mi = g % 2
                    yob, ryo = (PSB[2 + mi], rPS[2 + mi])
                    for h4 in range(4):
                        h = 4 * g + h4
                        P.op("PE", lambda e, h=h, h4=h4, c0=c0, c1=c1, yob=yob:
                             e.matmul(yob[:, h4 * 64:(h4 + 1) * 64],
                                      BCC[0:64, TQ + c0:TQ + c1],
                                      PREVB[0:64, h * PH:(h + 1) * PH],
                                      start=True, stop=True),
                             reads=(rBCC, rPREVB), writes=(ryo,), tag="yoff")

                def ycsc_grp(g):
                    mi = g % 2
                    yob, ryo = (PSB[2 + mi], rPS[2 + mi])
                    ysc = YSC[:, mi * 256:(mi + 1) * 256]
                    P.op("DVE", lambda e, g=g, yob=yob, ysc=ysc, tmc=tmc:
                         e.tensor_tensor(
                             ysc.rearrange("p (h j) -> p h j", j=PH),
                             yob[:, 0:256].rearrange("p (h j) -> p h j", j=PH),
                             tmc[:, 96 + 4 * g:100 + 4 * g].unsqueeze(2)
                             .broadcast_to([128, 4, PH]), ALU.mult),
                         reads=(ryo, rtmc), writes=(rYSl[mi],), tag="ycsc")


                def ydiag_grp(g):
                    mi = g % 3
                    gmap = GM[:, mi * 512:(mi + 1) * 512]
                    rgm = rGM[mi]
                    ydb, ryd = (PSB[g % 2], rPS[g % 2])
                    for h4 in range(4):
                        h = 4 * g + h4
                        P.op("PE", lambda e, h=h, h4=h4, gmap=gmap, ydb=ydb:
                             e.matmul(ydb[:, h4 * 64:(h4 + 1) * 64],
                                      gmap[:, h4 * 128:(h4 + 1) * 128],
                                      XDT[:, h * PH:(h + 1) * PH],
                                      start=True, stop=True),
                             reads=(rgm, rXDT), writes=(ryd,), tag="ydiag")

                def ycomb_grp(g):
                    mi = g % 2
                    ydb, ryd = (PSB[g % 2], rPS[g % 2])
                    ysc = YSC[:, mi * 256:(mi + 1) * 256]
                    P.op("DVE", lambda e, g=g, ydb=ydb, ysc=ysc: e.tensor_tensor(
                        YTM[:, g * 256:(g + 1) * 256], ydb[:, 0:256], ysc,
                        ALU.add),
                        reads=(ryd, rYSl[mi]), writes=(rYTM,), tag="ycomb")

                mask_front(0); mask_front(1); mask_front(2)
                mask_mid(0); mask_mid(1)
                yoff_grp(0)
                ycsc_grp(0)
                for g in range(8):
                    ydiag_grp(g)
                    if g + 3 < 8:
                        mask_front(g + 3)
                    if g + 2 < 8:
                        mask_mid(g + 2)
                    if g + 1 < 8:
                        yoff_grp(g + 1)
                        ycsc_grp(g + 1)
                    ycomb_grp(g)
                P.op("ACT", lambda e, c1=c1, cdr=cdr: e.activation(
                    cdr, DSTACK[96:128, c1 - 1:c1], AF.Copy),
                    reads=(rDST,), writes=(rcdr,), tag="cd")
                for q in range(4):
                    sb_, rsb = (PSB[4 + q % 2], rPS[4 + q % 2])
                    cb_, rcb = (PSB[4 + (q + 1) % 2], rPS[4 + (q + 1) % 2])
                    for hh in range(8):
                        h = q * 8 + hh
                        P.op("PE", lambda e, h=h, hh=hh, sb_=sb_, btm=btm: e.matmul(
                            sb_[0:64, hh * 64:(hh + 1) * 64],
                            btm, XDD[:, h * PH:(h + 1) * PH],
                            start=True, stop=True),
                            reads=(rbtm, rXDD), writes=(rsb,), tag="states")
                    P.op("PE", lambda e, q=q, cb_=cb_, cdr=cdr: e.matmul(
                        cb_[0:64, 0:512], cdr.broadcast_to([32, 64]),
                        SGRP[:, q * 512:(q + 1) * 512],
                        start=True, stop=True),
                        reads=(rcdr, rCONST), writes=(rcb,), tag="cdrep")
                    P.op("DVE", lambda e, q=q, cb_=cb_: e.tensor_tensor(
                        YSC[0:64, 0:512], PREV[0:64, q * 512:(q + 1) * 512],
                        cb_[0:64, 0:512], ALU.mult),
                        reads=(rPREV, rcb),
                        writes=(rYS, rYSl[0], rYSl[1], rYSl[2], rYSl[3]),
                        tag="scan1")
                    P.op("DVE", lambda e, q=q, sb_=sb_: e.tensor_tensor(
                        PREV[0:64, q * 512:(q + 1) * 512], YSC[0:64, 0:512],
                        sb_[0:64, 0:512], ALU.add),
                        reads=(rYS, rYSl[0], rsb), writes=(rPREV,), tag="scan2")
                    P.op("ACT", lambda e, q=q: e.activation(
                        PREVB[0:64, q * 512:(q + 1) * 512],
                        PREV[0:64, q * 512:(q + 1) * 512], AF.Copy),
                        reads=(rPREV,), writes=(rPREVB,), tag="prevb")
                for wv in range(2):
                    for dp in range(8):
                        d = wv * 8 + dp
                        P.op("PE", lambda e, d=d, dp=dp: e.transpose(
                            PBB[1][:, dp * 128:(dp + 1) * 128],
                            YTM[:, d * 128:(d + 1) * 128], IDB[:, :]),
                            reads=(rYTM, rCONST), writes=(rPB[1],), tag="yT")
                    P.op("DVE", lambda e, wv=wv, c0=c0: e.tensor_copy(
                        ARN[:, :].rearrange("p (t c) -> p t c", c=XW)
                        [:, wv * 8:(wv + 1) * 8, 4 + c0:4 + c0 + 128],
                        PBB[1][:, :].rearrange("p (t c) -> p t c", c=128)),
                        reads=(rPB[1],),
                        writes=(rARN,) + tuple(rARNl[wv * 8:wv * 8 + 8]),
                        tag="yTev2")
            for d in range(16):
                P.op("DVE", lambda e, d=d: e.scalar_tensor_tensor(
                    ARN[:, d * XW + 4:d * XW + 4 + TQ],
                    XBC[:, d * TQ:(d + 1) * TQ], DCOL[li][:, d:d + 1],
                    ARN[:, d * XW + 4:d * XW + 4 + TQ], ALU.mult, ALU.add),
                    reads=(rXBC, rARNl[d], rCONST),
                    writes=(rARN, rARNl[d]), tag="yD")
                P.op("DVE", lambda e, d=d: e.tensor_tensor(
                    ARN[:, d * XW + 4:d * XW + 4 + TQ],
                    ARN[:, d * XW + 4:d * XW + 4 + TQ],
                    SZ[:, d * TQ:(d + 1) * TQ], ALU.mult),
                    reads=(rARNl[d], rSZ), writes=(rARN, rARNl[d]), tag="ygate")
            dbg_store(f"ygate{li}_{qi}", ARN[:, :], (rARN,))
            for d in range(16):
                P.op("ACT", lambda e, d=d: e.activation(
                    SQT[:, 0:512], ARN[:, d * XW + 4:d * XW + 4 + TQ],
                    AF.Square),
                    reads=(rARNl[d],), writes=(rSQT,), tag="yrsq")
                P.op("PE", lambda e, d=d: e.matmul(
                    PSB[2][0:1, 0:512], ONESB[:, 0:1], SQT[:, 0:512],
                    start=(d == 0), stop=(d == 15)),
                    reads=(rSQT, rCONST), writes=(rPS[2],), tag="yrmm")
            P.op("DVE", lambda e: e.tensor_copy(STAT[0:1, 0:512], PSB[2][0:1, 0:512]),
                 reads=(rPS[2],), writes=(rSTAT,), tag="yrev")
            rownorm_istd(DI, "yr")
            rank1_bcast(STATB[0:1, 512:1024], 3, (rSTAT, rYS), "yrbc")
            for d in range(16):
                P.op("DVE", lambda e, d=d: e.scalar_tensor_tensor(
                    ARN[:, d * XW + 4:d * XW + 4 + TQ],
                    ARN[:, d * XW + 4:d * XW + 4 + TQ],
                    RMSW[li][:, d:d + 1], PSB[3][:, 0:512], ALU.mult, ALU.mult),
                    reads=(rARNl[d], rPS[3], rCONST),
                    writes=(rARN, rARNl[d]), tag="yrap")

            for ob in range(8):
                wb, rwb = (WA, rWA) if ob % 2 == 0 else (WB, rWB)
                wblk_load(wb, rwb, outw, ob * DI * 128, DI, 128, tag=f"outw{ob}")
                pb, rpb = (PSB[0], rPS[0]) if ob % 2 == 0 else (PSB[1], rPS[1])
                for k in range(16):
                    P.op("PE", lambda e, k=k, wb=wb, pb=pb: e.matmul(
                        pb[:, 0:512], wb[:, k * 128:(k + 1) * 128],
                        ARN[:, k * XW + 4:k * XW + 4 + 512],
                        start=(k == 0), stop=(k == 15)),
                        reads=(rwb, rARNl[k]), writes=(rpb,), tag="outmm")
                P.op("DVE", lambda e, ob=ob, t0=t0, pb=pb: e.tensor_tensor(
                    RESID[:, ob * T + t0:ob * T + t0 + 512],
                    RESID[:, ob * T + t0:ob * T + t0 + 512],
                    pb[:, 0:512], ALU.add),
                    reads=(rRQ[qi], rpb), writes=(rRQ[qi],), tag="outev")
            dbg_store(f"res{li}_{qi}", RESID[:, :], rRESall)

    P.dma("SP", "usp", lambda e: e.dma_start(
        uspill.ap().rearrange("(a p) t -> p a t", p=128),
        RESID[:, :].rearrange("p (a t) -> p a t", t=T)),
        reads=rRESall, writes=(rUSP,), tag="uspill")

    def ln1_dst(k, t0, tp):
        P.op("DVE", lambda e, k=k, t0=t0: e.tensor_tensor(
            RESID[:, k * T + t0:k * T + t0 + 512], ULD[:, 0:512],
            PSB[3][:, 0:512], ALU.mult),
            reads=(rULD, rPS[3], rUSP), writes=(rRQ[tp],), tag="ln1w")

    layernorm(ln1_dst, "ln1")
    dbg_store("ln1", RESID[:, :], rRESall)

    attn_layer(0)
    attn_layer(1)

    for k in range(N_DMT):
        for hp in range(4):
            P.dma("SP", "uld", lambda e, k=k, hp=hp: e.dma_start(
                ULD[:, :], bass.AP(uspill, k * 128 * T + hp * 512,
                                   [[T, 128], [1, 512]])),
                reads=(rUSP,), writes=(rULD,), tag="uld")
            P.op("DVE", lambda e, k=k, hp=hp: e.tensor_tensor(
                RESID[:, k * T + hp * 512:k * T + (hp + 1) * 512],
                RESID[:, k * T + hp * 512:k * T + (hp + 1) * 512],
                ULD[:, :], ALU.add),
                reads=(rRQ[hp], rULD), writes=(rRQ[hp],), tag="x1")
    dbg_store("x1", RESID[:, :], rRESall)

    rBAR = res("attn_done")
    P.op("DVE", lambda e: e.engine_nop(),
         reads=(rPREV, rPREVB, rCDR, rCDROW, rBTM, rGTM, rGM[0], rGM[1],
                rXDT, rXDD, rYTM, rTMC, rDST, rDA, rSZ, rXBC, rARN, rNRM,
                rDIAG, rBCRAW, rBCC, rDGBC),
         writes=(rBAR,), tag="attnbar")

    for nm in reversed(("SZ", "XBC", "ARN", "NRM", "DIAG", "BCRAW", "BCC",
                        "DGBC", "DSTACK", "TMC", "XDT", "XDD", "YTM",
                        "GM", "GTM", "BTM", "PREV", "PREVB", "CDR", "CDROW")):
        sbuf_free(nm)
    LN2B = sbuf("LN2B", [128, N_DMT * T], BF16, side="right")
    ABUF = sbuf("ABUF", [128, N_DHT * TF], BF16, side="right")
    WH = [sbuf(f"WH{i}", [128, 32 * 128], BF16, side="right") for i in range(2)]
    rLN2 = res("LN2B"); rAB = res("ABUF"); rWH = [res("WH0"), res("WH1")]
    rABl = [res(f"ABUF{i}") for i in range(8)]
    rGBl = [res(f"GB{i}") for i in range(8)]
    GBUF = RESID
    rGB = rRES

    def ln2_dst(k, t0, tp):
        P.op("DVE", lambda e, k=k, t0=t0: e.tensor_tensor(
            LN2B[:, k * T + t0:k * T + t0 + 512], ULD[:, 0:512],
            PSB[3][:, 0:512], ALU.mult),
            reads=(rULD, rPS[3], rUSP, rBAR), writes=(rLN2,), tag="ln2w")

    layernorm(ln2_dst, "ln2")
    dbg_store("ln2", LN2B[:, :], (rLN2,))

    P.dma("SP", "usp", lambda e: e.dma_start(
        uspill.ap().rearrange("(a p) t -> p a t", p=128),
        RESID[:, :].rearrange("p (a t) -> p a t", t=T)),
        reads=rRESall, writes=(rUSP,), tag="x1spill")

    def gsl(dt_, c0, n=512):
        return GBUF[:, :].bitcast(BF16)[:, dt_ * TF + c0:dt_ * TF + c0 + n]

    for ch in range(2):
        ct0 = ch * TF
        for stage in range(2):
            wnm, snm = (("w1", "s1"), ("w2", "s2"))[stage]
            for obp in range(N_DHT * 2):
                ob = obp
                if ob >= N_DHT:
                    break
                if ob % 2 == 0:
                    wb, rwb = (WA, rWA) if (ob // 2) % 2 == 0 else (WB, rWB)
                    src_ap = bass.AP(dr[wnm], ob * DM * 128,
                                     [[128, 128], [DM * 128, 2], [128 * 128, 8],
                                      [1, 128]])
                    dst_ap = wb[:, 0:2048].rearrange("p (b k o) -> p b k o",
                                                     b=2, o=128)
                    P.dma("POOL", "wb_" + rwb.name,
                          lambda e, d=dst_ap, s=src_ap: e.dma_start(d, s),
                          reads=(rUSP, rBAR), writes=(rwb,), tag=wnm)
                wb, rwb = (WA, rWA) if (ob // 2) % 2 == 0 else (WB, rWB)
                boff = (ob % 2) * 1024
                for nh in range(2):
                    pb, rpb = (PSB[0], rPS[0]) if nh == 0 else (PSB[1], rPS[1])
                    c0 = nh * 512
                    for k in range(N_DMT):
                        P.op("PE", lambda e, k=k, c0=c0, ct0=ct0, wb=wb, pb=pb,
                             boff=boff:
                             e.matmul(pb[:, 0:512],
                                      wb[:, boff + k * 128:boff + (k + 1) * 128],
                                      LN2B[:, k * T + ct0 + c0:k * T + ct0 + c0 + 512],
                                      start=(k == 0), stop=(k == N_DMT - 1)),
                             reads=(rwb, rLN2), writes=(rpb,), tag=wnm + "mm")
                    if stage == 0:
                        P.op("ACT", lambda e, ob=ob, c0=c0, pb=pb: e.activation(
                            ABUF[:, ob * TF + c0:ob * TF + c0 + 512], pb[:, 0:512],
                            AF.Silu, scale=FSC["s1"][:, ob:ob + 1]),
                            reads=(rpb, rCONST, rUSP, rBAR),
                            writes=(rAB, rABl[ob // 4]), tag="aev")
                    else:
                        P.op("DVE", lambda e, ob=ob, c0=c0, pb=pb:
                             e.scalar_tensor_tensor(
                                 ABUF[:, ob * TF + c0:ob * TF + c0 + 512],
                                 pb[:, 0:512], FSC["s2"][:, ob:ob + 1],
                                 ABUF[:, ob * TF + c0:ob * TF + c0 + 512],
                                 ALU.mult, ALU.mult),
                             reads=(rpb, rCONST, rABl[ob // 4]),
                             writes=(rAB, rABl[ob // 4]), tag="hev")
        for stage in range(2):
            wnm, snm = (("h1", "sh1"), ("h2", "sh2"))[stage]
            for ob in range(N_DHT):
                wb, rwb = WH[ob % 2], rWH[ob % 2]
                wblk_load(wb, rwb, dr[wnm], ob * DH * 128, DH, 128,
                          extra_reads=(rUSP, rBAR), tag=wnm)
                for nh in range(2):
                    pb, rpb = (PSB[0], rPS[0]) if nh == 0 else (PSB[1], rPS[1])
                    c0 = nh * 512
                    for k in range(N_DHT):
                        if stage == 0:
                            rhs = ABUF[:, k * TF + c0:k * TF + c0 + 512]
                            rr = rABl[k // 4]
                        else:
                            rhs = gsl(k, c0)
                            rr = rGBl[k // 4]
                        P.op("PE", lambda e, k=k, wb=wb, pb=pb, rhs=rhs: e.matmul(
                            pb[:, 0:512], wb[:, k * 128:(k + 1) * 128], rhs,
                            start=(k == 0), stop=(k == N_DHT - 1)),
                            reads=(rwb, rr), writes=(rpb,), tag=wnm + "mm")
                    if stage == 0:
                        P.op("ACT", lambda e, ob=ob, c0=c0, pb=pb: e.activation(
                            gsl(ob, c0), pb[:, 0:512],
                            AF.Gelu, scale=FSC["sh1"][:, ob:ob + 1]),
                            reads=(rpb, rCONST),
                            writes=rRESall + (rGBl[ob // 4],), tag="gev")
                    else:
                        P.op("ACT", lambda e, ob=ob, c0=c0, pb=pb: e.activation(
                            ABUF[:, ob * TF + c0:ob * TF + c0 + 512], pb[:, 0:512],
                            AF.Gelu, scale=FSC["sh2"][:, ob:ob + 1]),
                            reads=(rpb, rCONST),
                            writes=(rAB, rABl[ob // 4]), tag="g2ev")
        for ob in range(N_DMT):
            wb, rwb = WH[ob % 2], rWH[ob % 2]
            wblk_load(wb, rwb, dr["w3"], ob * DH * 128, DH, 128,
                      extra_reads=(rUSP, rBAR), tag="w3")

            for nh in range(2):
                pb, rpb = (PSB[0], rPS[0]) if nh == 0 else (PSB[1], rPS[1])
                ev, rev = (YSC, rYS) if nh == 0 else (None, rMS)
                c0 = nh * 512
                P.dma("SP", "uld", lambda e, ob=ob, c0=c0, ct0=ct0:
                      e.dma_start(ULD[:, 0:512],
                                  bass.AP(uspill, ob * 128 * T + ct0 + c0,
                                          [[T, 128], [1, 512]])),
                      reads=(rUSP,), writes=(rULD,), tag="x1ld")
                for k in range(N_DHT):
                    P.op("PE", lambda e, k=k, c0=c0, wb=wb, pb=pb: e.matmul(
                        pb[:, 0:512], wb[:, k * 128:(k + 1) * 128],
                        ABUF[:, k * TF + c0:k * TF + c0 + 512],
                        start=(k == 0), stop=(k == N_DHT - 1)),
                        reads=(rwb, rABl[k // 4]), writes=(rpb,), tag="w3mm")
                evap = (YSC[:, 0:512] if nh == 0
                        else ESCR[:, :].bitcast(F32)[:, 0:512])
                P.op("DVE", lambda e, ob=ob, c0=c0, pb=pb, evap=evap:
                     e.scalar_tensor_tensor(
                         evap, pb[:, 0:512], FSC["s3"][:, ob:ob + 1],
                         ULD[:, 0:512],
                         ALU.mult, ALU.add),
                     reads=(rpb, rULD, rCONST),
                     writes=((rYS, rYSl[0], rYSl[1], rYSl[2], rYSl[3])
                             if nh == 0 else (rMS, rMS2, rES)), tag="w3ev")
                P.dma("SP", "out", lambda e, ob=ob, c0=c0, ct0=ct0, evap=evap:
                      e.dma_start(bass.AP(y_out, ob * 128 * T + ct0 + c0,
                                          [[T, 128], [1, 512]]), evap),
                      reads=((rYS,) if nh == 0 else (rMS,)), writes=(rOUT,),
                      tag="ystore")

    P.finalize()
    return nc, P



def emit(nc, P):
    from contextlib import ExitStack
    st = ExitStack()
    sems = {}
    for e in Prog.ENGS:
        sems[("E", e)] = st.enter_context(nc.semaphore("se_" + e))
    for s in P.dma_sems:
        sems[("D", s)] = st.enter_context(nc.semaphore("sd_" + s))
    block = st.enter_context(nc.Block())

    by_eng = {e: [] for e in Prog.ENGS}
    for o, w in zip(P.ops, P.waits):
        by_eng[o.eng].append((o, w))

    def run_stream(eng_obj, eng_name):
        for o, w in by_eng[eng_name]:
            for key, val in w.items():
                eng_obj.wait_ge(sems[key], val)
            inst = o.fn(eng_obj)
            if o.is_dma:
                inst.then_inc(sems[("D", o.sem)], 16)
            elif o.need_inc:
                inst.then_inc(sems[("E", o.eng)], 1)
        if eng_name == "SP":
            eng_obj.wait_ge(sems[("D", "out")], P.dma_sems["out"])
            if "dbgout" in P.dma_sems:
                eng_obj.wait_ge(sems[("D", "dbgout")], P.dma_sems["dbgout"])

    @block.tensor
    def _(e):
        run_stream(e, "PE")

    @block.vector
    def _(e):
        run_stream(e, "DVE")

    @block.scalar
    def _(e):
        run_stream(e, "ACT")

    @block.gpsimd
    def _(e):
        run_stream(e, "POOL")

    @block.sync
    def _(e):
        run_stream(e, "SP")

    st.close()


_CACHE = {}


def _get_program(debug_taps=()):
    key = tuple(debug_taps)
    if key not in _CACHE:
        nc, P = build_program(debug_taps)
        emit(nc, P)
        _CACHE[key] = nc
    return _CACHE[key]


def kernel(**inputs):
    per_core = _prep_inputs(inputs)
    nc = _get_program()
    res = run_bass_kernel_spmd(nc, per_core, core_ids=list(range(NCORES)))
    out = np.stack([np.ascontiguousarray(np.asarray(r["y"]).T)
                    for r in res.results])
    return out.astype(np.float32)
